# version 12
# baseline (speedup 1.0000x reference)
import numpy as np
from contextlib import ExitStack

import concourse.bass as bass
import concourse.mybir as mybir
from concourse.bass_utils import run_bass_kernel_spmd

F32 = mybir.dt.float32
BF16 = mybir.dt.bfloat16
AF = mybir.ActivationFunctionType
ALU = mybir.AluOpType

T = 4096
DM = 2048
D = 128
HPC = 4
NT = T // 128
NQ = T // 512
TS = 1024
NTT = TS // 128
EPS = 1e-5
EPS_F = 1e-7

NKQ = [4 * qt + 4 for qt in range(NQ)]
SPH = sum(NKQ)
TS_AFTER_R = []
_c = 0
for _h in range(HPC):
    for _qt in range(NQ):
        _c += NKQ[_qt]
        TS_AFTER_R.append(_c)


class Ctr:
    __slots__ = ("sem", "n")

    def __init__(self, sem):
        self.sem = sem
        self.n = 0

    def inc(self, inst, k=1):
        inst.then_inc(self.sem, k)
        self.n += k
        return self.n


def build_nc():
    nc = bass.Bass("TRN2", target_bir_lowering=False)

    x_in = nc.dram_tensor("x", [T, DM], F32, kind="ExternalInput").ap()
    wqt_in = nc.dram_tensor("wqt", [DM, 512], F32, kind="ExternalInput").ap()
    wkt_in = nc.dram_tensor("wkt", [DM, 512], F32, kind="ExternalInput").ap()
    wvt_in = nc.dram_tensor("wvt", [DM, 512], F32, kind="ExternalInput").ap()
    wot_in = nc.dram_tensor("wot", [512, DM], F32, kind="ExternalInput").ap()
    wf_in = nc.dram_tensor("wf", [DM, DM], F32, kind="ExternalInput").ap()
    gam_in = nc.dram_tensor("gam", [1, D], F32, kind="ExternalInput").ap()
    bet_in = nc.dram_tensor("bet", [1, D], F32, kind="ExternalInput").ap()
    fsc_in = nc.dram_tensor("fsc", [1, DM], F32, kind="ExternalInput").ap()
    id_in = nc.dram_tensor("ident", [D, D], F32, kind="ExternalInput").ap()
    mk_in = nc.dram_tensor("masks", [D, 2048], F32, kind="ExternalInput").ap()
    out = nc.dram_tensor("out", [TS, DM], F32, kind="ExternalOutput").ap()

    qTh_sp = nc.dram_tensor("qTh_sp", [HPC, D, T], BF16).ap()
    qTl_sp = nc.dram_tensor("qTl_sp", [HPC, D, T], BF16).ap()
    kTh_sp = nc.dram_tensor("kTh_sp", [HPC, D, T], BF16).ap()
    kTl_sp = nc.dram_tensor("kTl_sp", [HPC, D, T], BF16).ap()
    v_sp = nc.dram_tensor("v_sp", [HPC, T, D], F32).ap()
    onh_sp = nc.dram_tensor("onh_sp", [T, 512], BF16).ap()
    onl_sp = nc.dram_tensor("onl_sp", [T, 512], BF16).ap()
    agin = nc.dram_tensor("agin", [4, TS, DM], F32).ap()
    rsout = nc.dram_tensor("rsout", [4, TS // 4, DM], F32).ap()
    us = nc.dram_tensor("us", [TS, DM], F32).ap()

    groups = [[0, 1, 2, 3], [4, 5, 6, 7]]

    with ExitStack() as g:
        E = g.enter_context

        ident = E(nc.sbuf_tensor("identsb", [D, D], F32))
        s1 = [E(nc.sbuf_tensor(f"s1_{j}", [128, DM], F32)) for j in range(2)]
        s2 = [E(nc.sbuf_tensor(f"s2_{j}", [128, DM], F32)) for j in range(2)]
        s3 = [E(nc.sbuf_tensor(f"s3_{j}", [128, DM], F32)) for j in range(2)]
        st24 = E(nc.sbuf_tensor("st24", [128, 24], F32))
        mvln = E(nc.sbuf_tensor("mvln", [128, 2], F32))
        vpeln = E(nc.sbuf_tensor("vpeln", [128, 1], F32))
        sdln = E(nc.sbuf_tensor("sdln", [128, 1], F32))
        rsln = E(nc.sbuf_tensor("rsln", [128, 1], F32))
        sc4 = E(nc.sbuf_tensor("sc4", [128, 32], F32))
        fsc1 = E(nc.sbuf_tensor("fsc1", [128, 16], F32))

        names = [
            "sC", "sW", "s1l0", "s1l1", "s2l0", "s2l1", "vGF",
            "sWq0", "sWq1", "vLNH",
            "vST", "vAG", "vVPE", "vVPq", "vVPk", "sSD", "vRC", "vLNX",
            "tTP", "sEV", "tPJ", "sVE",
            "vYQ", "vSTq", "vAGq", "sSDq", "vRCq", "vFMq",
            "vYK", "vSTk", "vAGk", "sSDk", "vRCk", "vFMk",
            "gSP0", "gSP1",
            "sKV", "sQ0", "sQ1", "tS", "sSQ", "vMK", "tOM", "vRZ",
            "sOE", "sOEb", "gO0", "gO1", "vMS",
            "sWO", "tM25", "vPR", "gPR0", "gPR1", "cCC",
            "wl0", "wl1", "vCH", "sCH", "tM4", "vEV4",
            "gX0", "gX1", "gU0", "gU1", "gOUT", "vH2", "tTP2", "sEV2",
        ]
        C = {n: Ctr(E(nc.semaphore(n))) for n in names}
        s1l = [C["s1l0"], C["s1l1"]]
        s2l = [C["s2l0"], C["s2l1"]]
        gSP = [C["gSP0"], C["gSP1"]]
        sQ = [C["sQ0"], C["sQ1"]]
        gO = [C["gO0"], C["gO1"]]
        gPR = [C["gPR0"], C["gPR1"]]
        wl = [C["wl0"], C["wl1"]]
        gX = [C["gX0"], C["gX1"]]
        gU = [C["gU0"], C["gU1"]]

        block = E(nc.Block())

        with ExitStack() as p1:
            P = p1.enter_context
            wqh = P(nc.sbuf_tensor("wqh", [128, 16 * 512], BF16))
            wql = P(nc.sbuf_tensor("wql", [128, 16 * 512], BF16))
            wkh = P(nc.sbuf_tensor("wkh", [128, 16 * 512], BF16))
            wkl = P(nc.sbuf_tensor("wkl", [128, 16 * 512], BF16))
            wvh = P(nc.sbuf_tensor("wvh", [128, 16 * 512], BF16))
            wvl = P(nc.sbuf_tensor("wvl", [128, 16 * 512], BF16))
            grep = P(nc.sbuf_tensor("grep", [128, 512], F32))
            brep = P(nc.sbuf_tensor("brep", [128, 512], F32))
            identb = P(nc.sbuf_tensor("identb", [D, D], BF16))
            lnxTh = [P(nc.sbuf_tensor(f"lnxTh{j}", [128, DM], BF16)) for j in range(2)]
            lnxTl = [P(nc.sbuf_tensor(f"lnxTl{j}", [128, DM], BF16)) for j in range(2)]
            lnxh = [P(nc.sbuf_tensor(f"lnxh{j}", [128, DM], BF16)) for j in range(2)]
            lnxl = [P(nc.sbuf_tensor(f"lnxl{j}", [128, DM], BF16)) for j in range(2)]
            yq = P(nc.sbuf_tensor("yq", [128, 512], F32))
            yk = P(nc.sbuf_tensor("yk", [128, 512], F32))
            qhh = [P(nc.sbuf_tensor(f"qhh{j}", [128, 512], BF16)) for j in range(2)]
            qhl = [P(nc.sbuf_tensor(f"qhl{j}", [128, 512], BF16)) for j in range(2)]
            khh = [P(nc.sbuf_tensor(f"khh{j}", [128, 512], BF16)) for j in range(2)]
            khl = [P(nc.sbuf_tensor(f"khl{j}", [128, 512], BF16)) for j in range(2)]
            qhTh = [P(nc.sbuf_tensor(f"qhTh{j}", [128, 512], BF16)) for j in range(2)]
            qhTl = [P(nc.sbuf_tensor(f"qhTl{j}", [128, 512], BF16)) for j in range(2)]
            khTh = [P(nc.sbuf_tensor(f"khTh{j}", [128, 512], BF16)) for j in range(2)]
            khTl = [P(nc.sbuf_tensor(f"khTl{j}", [128, 512], BF16)) for j in range(2)]
            vsb = [P(nc.sbuf_tensor(f"vsb{j}", [128, 512], F32)) for j in range(2)]
            mvq = P(nc.sbuf_tensor("mvq", [128, 8], F32))
            mvk = P(nc.sbuf_tensor("mvk", [128, 8], F32))
            vpe4q = P(nc.sbuf_tensor("vpe4q", [128, 4], F32))
            vpe4k = P(nc.sbuf_tensor("vpe4k", [128, 4], F32))
            sd4q = P(nc.sbuf_tensor("sd4q", [128, 4], F32))
            sd4k = P(nc.sbuf_tensor("sd4k", [128, 4], F32))
            rs4q = P(nc.sbuf_tensor("rs4q", [128, 4], F32))
            rs4k = P(nc.sbuf_tensor("rs4k", [128, 4], F32))
            pq = [P(nc.psum_tensor(f"pq{j}", [128, 512], F32)) for j in range(2)]
            pk = [P(nc.psum_tensor(f"pk{j}", [128, 512], F32)) for j in range(2)]
            pv = [P(nc.psum_tensor(f"pv{j}", [128, 512], F32)) for j in range(2)]
            pT = [P(nc.psum_tensor(f"pT{j}", [128, 512], F32)) for j in range(2)]

            SCHED = [("T", 0), ("T", 1)]
            for _i in range(NT):
                if _i >= 1:
                    SCHED.append(("fm", _i - 1))
                if _i <= NT - 3:
                    SCHED.append(("T", _i + 2))
            SCHED.append(("fm", NT - 1))
            POS_T = {}
            POS_FM = {}
            _cum = 0
            for _kind, _idx in SCHED:
                _n = 32 if _kind == "T" else 16
                if _kind == "T":
                    POS_T[_idx] = (_cum + 1, _cum + _n)
                else:
                    POS_FM[_idx] = (_cum + 1, _cum + _n)
                _cum += _n
            TTP_TOTAL = _cum

            VGF_SUB = []
            _v = 1
            for _qi in range(12):
                _v += (4 if _qi < 8 else 0) + 3
                VGF_SUB.append(_v)
                _v += 1
            VGF_TOT = _v

            @block.sync
            def _(sync):
                for dst_, src_ in ((ident, id_in),):
                    sync.dma_start(out=dst_[:], in_=src_[:]).then_inc(C["sC"].sem, 16)
                    C["sC"].n += 16
                for r in range(4):
                    sync.dma_start(out=grep[:, 128 * r:128 * (r + 1)],
                                   in_=gam_in.partition_broadcast(128)
                                   ).then_inc(C["sC"].sem, 16)
                    C["sC"].n += 16
                    sync.dma_start(out=brep[:, 128 * r:128 * (r + 1)],
                                   in_=bet_in.partition_broadcast(128)
                                   ).then_inc(C["sC"].sem, 16)
                    C["sC"].n += 16
                wins = [wqt_in, wkt_in, wvt_in]
                sWq = [C["sWq0"], C["sWq1"]]
                for qi in range(12):
                    w, q = qi // 4, qi % 4
                    j = qi % 2
                    if qi >= 2:
                        sync.wait_ge(C["vGF"].sem, VGF_SUB[qi - 2])
                    sync.dma_start(
                        out=s3[j].ap().rearrange("p (icq c) -> p icq c", c=512),
                        in_=wins[w].rearrange("(ic p) c -> p ic c",
                                              p=128)[:, 4 * q:4 * (q + 1), :],
                    ).then_inc(sWq[j].sem, 16)
                    sWq[j].n += 16
                for i in range(NT):
                    j = i % 2
                    if i >= 2:
                        sync.wait_ge(C["vLNX"].sem, i - 1)
                    sync.dma_start(out=s1[j][:], in_=x_in[128 * i:128 * (i + 1), :]
                                   ).then_inc(s1l[j].sem, 16)
                    s1l[j].n += 16

            @block.vector
            def _(vector):
                vector.wait_ge(C["sC"].sem, C["sC"].n)
                C["vGF"].inc(vector.tensor_copy(identb[:], ident[:]))
                sWq = [C["sWq0"], C["sWq1"]]
                Whl = [(wqh, wql), (wkh, wkl), (wvh, wvl)]
                for qi in range(12):
                    w, q = qi // 4, qi % 4
                    j = qi % 2
                    hi_t, lo_t = Whl[w]
                    vector.wait_ge(sWq[j].sem, 16 * (qi // 2 + 1))
                    if qi < 8:
                        for c4 in range(4):
                            C["vGF"].inc(vector.tensor_mul(
                                s3[j][:, 512 * c4:512 * (c4 + 1)],
                                s3[j][:, 512 * c4:512 * (c4 + 1)], grep[:]))
                        vector.wait_ge(C["vGF"].sem, C["vGF"].n)
                    C["vGF"].inc(vector.tensor_copy(
                        hi_t[:, 2048 * q:2048 * (q + 1)], s3[j][:]))
                    vector.wait_ge(C["vGF"].sem, C["vGF"].n)
                    C["vGF"].inc(vector.tensor_copy(
                        s2[j][:], hi_t[:, 2048 * q:2048 * (q + 1)]))
                    vector.wait_ge(C["vGF"].sem, C["vGF"].n)
                    assert C["vGF"].n + 1 == VGF_SUB[qi] + 1 or True
                    C["vGF"].inc(vector.tensor_sub(s2[j][:], s3[j][:], s2[j][:]))
                    assert C["vGF"].n == VGF_SUB[qi], (qi, C["vGF"].n, VGF_SUB[qi])
                    vector.wait_ge(C["vGF"].sem, C["vGF"].n)
                    C["vGF"].inc(vector.tensor_copy(
                        lo_t[:, 2048 * q:2048 * (q + 1)], s2[j][:]))
                assert C["vGF"].n == VGF_TOT, (C["vGF"].n, VGF_TOT)

                def emit_ln(n):
                    jn = n % 2
                    xt = s1[jn]
                    vector.wait_ge(s1l[jn].sem, 16 * (n // 2 + 1))
                    for q4 in range(3):
                        vector.bn_stats(st24[:, 6 * q4:6 * (q4 + 1)],
                                        xt[:, 512 * q4:512 * (q4 + 1)])
                    C["vST"].inc(vector.bn_stats(st24[:, 18:24], xt[:, 1536:2048]))
                    vector.wait_ge(C["vST"].sem, C["vST"].n)
                    C["vAG"].inc(vector.bn_aggr(mvln[:], st24[:]))
                    vector.wait_ge(C["vAG"].sem, C["vAG"].n)
                    C["vVPE"].inc(vector.tensor_scalar(vpeln[:], mvln[:, 1:2],
                                                       EPS, None, ALU.add))
                    vector.wait_ge(C["sSD"].sem, n + 1)
                    C["vRC"].inc(vector.reciprocal(rsln[:], sdln[:]))
                    vector.wait_ge(C["vRC"].sem, C["vRC"].n)
                    C["vLNX"].inc(vector.tensor_scalar(
                        s2[jn][:], xt[:], mvln[:, 0:1], rsln[:],
                        ALU.subtract, ALU.mult))
                    vector.wait_ge(C["vLNX"].sem, C["vLNX"].n)
                    if n >= 2:
                        vector.wait_ge(C["tTP"].sem, POS_T[n - 2][1])
                    C["vLNH"].inc(vector.tensor_copy(lnxh[jn][:], s2[jn][:]))
                    vector.wait_ge(C["vLNH"].sem, C["vLNH"].n)
                    C["vLNH"].inc(vector.tensor_copy(s3[jn][:], lnxh[jn][:]))
                    vector.wait_ge(C["vLNH"].sem, C["vLNH"].n)
                    C["vLNH"].inc(vector.tensor_sub(s3[jn][:], s2[jn][:], s3[jn][:]))
                    vector.wait_ge(C["vLNH"].sem, C["vLNH"].n)
                    C["vLNH"].inc(vector.tensor_copy(lnxl[jn][:], s3[jn][:]))

                def emit_fm(i, which):
                    j = i % 2
                    if which == "q":
                        ps_, y_, mv_, vpe_, sd_, rs_ = \
                            pq[j], yq, mvq, vpe4q, sd4q, rs4q
                        hat_ = s3[j][:, 512:1024]
                        cY, cST, cAG, cVP, cSD, cRC, cFM = (
                            C["vYQ"], C["vSTq"], C["vAGq"], C["vVPq"],
                            C["sSDq"], C["vRCq"], C["vFMq"])
                        off = 1
                        scl = 128.0
                    else:
                        ps_, y_, mv_, vpe_, sd_, rs_ = \
                            pk[j], yk, mvk, vpe4k, sd4k, rs4k
                        hat_ = s3[j][:, 1024:1536]
                        cY, cST, cAG, cVP, cSD, cRC, cFM = (
                            C["vYK"], C["vSTk"], C["vAGk"], C["vVPk"],
                            C["sSDk"], C["vRCk"], C["vFMk"])
                        off = 2
                        scl = None
                    vector.wait_ge(C["tPJ"].sem, 3 * i + off)
                    cY.inc(vector.tensor_add(y_[:], ps_[:], brep[:]))
                    vector.wait_ge(cY.sem, cY.n)
                    ins = None
                    for h in range(4):
                        ins = vector.bn_stats(st24[:, 6 * h:6 * (h + 1)],
                                              y_[:, 128 * h:128 * (h + 1)])
                    cST.inc(ins)
                    vector.wait_ge(cST.sem, cST.n)
                    for h in range(4):
                        ins = vector.bn_aggr(mv_[:, 2 * h:2 * (h + 1)],
                                             st24[:, 6 * h:6 * (h + 1)])
                    cAG.inc(ins)
                    vector.wait_ge(cAG.sem, cAG.n)
                    if scl is not None:
                        cVP.inc(vector.tensor_scalar(
                            vpe_[:],
                            mv_.ap().rearrange("p (h t) -> p h t", t=2)[:, :, 1:2],
                            EPS, scl, ALU.add, ALU.mult))
                    else:
                        cVP.inc(vector.tensor_scalar(
                            vpe_[:],
                            mv_.ap().rearrange("p (h t) -> p h t", t=2)[:, :, 1:2],
                            EPS, None, ALU.add))
                    vector.wait_ge(cSD.sem, i + 1)
                    cRC.inc(vector.reciprocal(rs_[:], sd_[:]))
                    vector.wait_ge(cRC.sem, cRC.n)
                    if which == "q" and i >= 2:
                        vector.wait_ge(C["tTP"].sem, POS_FM[i - 2][1])
                    for h in range(4):
                        ins = vector.tensor_scalar(
                            hat_[:, 128 * h:128 * (h + 1)],
                            y_[:, 128 * h:128 * (h + 1)],
                            mv_[:, 2 * h:2 * h + 1], rs_[:, h:h + 1],
                            ALU.subtract, ALU.mult)
                    cRC.inc(ins)
                    vector.wait_ge(cRC.sem, cRC.n)
                    hi_b = qhh[j] if which == "q" else khh[j]
                    lo_b = qhl[j] if which == "q" else khl[j]
                    scr = s3[j][:, 1536:2048]
                    cRC.inc(vector.tensor_copy(hi_b[:], hat_[:]))
                    vector.wait_ge(cRC.sem, cRC.n)
                    cRC.inc(vector.tensor_copy(scr, hi_b[:]))
                    vector.wait_ge(cRC.sem, cRC.n)
                    cRC.inc(vector.tensor_sub(scr, hat_[:], scr))
                    vector.wait_ge(cRC.sem, cRC.n)
                    cFM.inc(vector.tensor_copy(lo_b[:], scr))

                emit_ln(0)
                emit_ln(1)
                for i in range(NT):
                    emit_fm(i, "q")
                    emit_fm(i, "k")
                    if i + 2 < NT:
                        emit_ln(i + 2)

            @block.scalar
            def _(scalar):
                def ev_fm(m):
                    jm = m % 2
                    if m >= 2:
                        scalar.wait_ge(gSP[jm].sem, 80 * (m // 2))
                    dsts = (qhTh[jm], qhTl[jm], khTh[jm], khTl[jm])
                    p0 = POS_FM[m][0]
                    for hh in range(16):
                        tp_pos = p0 + hh
                        scalar.wait_ge(C["tTP"].sem, tp_pos)
                        dst = dsts[hh // 4]
                        col = 128 * (hh % 4)
                        C["sEV"].inc(scalar.copy(
                            out=dst[:, col:col + 128],
                            in_=pT[(tp_pos - 1) % 2].ap().bitcast(BF16)[:, 0:128]))

                def ev_T(n):
                    jn = n % 2
                    if n >= 2:
                        scalar.wait_ge(C["tPJ"].sem, 3 * (n - 2) + 3)
                    p0 = POS_T[n][0]
                    for k in range(32):
                        tp_pos = p0 + k
                        scalar.wait_ge(C["tTP"].sem, tp_pos)
                        dstT = lnxTh[jn] if k < 16 else lnxTl[jn]
                        kc = k % 16
                        C["sEV"].inc(scalar.copy(
                            out=dstT[:, 128 * kc:128 * (kc + 1)],
                            in_=pT[(tp_pos - 1) % 2].ap().bitcast(BF16)[:, 0:128]))

                def sd_ln(n):
                    scalar.wait_ge(C["vVPE"].sem, n + 1)
                    C["sSD"].inc(scalar.activation(out=sdln[:], in_=vpeln[:],
                                                   func=AF.Sqrt))

                sd_ln(0)
                sd_ln(1)
                ev_T(0)
                ev_T(1)
                for i in range(NT):
                    j = i % 2
                    scalar.wait_ge(C["vVPq"].sem, i + 1)
                    C["sSDq"].inc(scalar.activation(
                        out=sd4q[:], in_=vpe4q[:], func=AF.Sqrt))
                    scalar.wait_ge(C["vVPk"].sem, i + 1)
                    C["sSDk"].inc(scalar.activation(
                        out=sd4k[:], in_=vpe4k[:], func=AF.Sqrt))
                    if i + 2 < NT:
                        sd_ln(i + 2)
                    if i >= 1:
                        ev_fm(i - 1)
                    scalar.wait_ge(C["tPJ"].sem, 3 * i + 3)
                    if i >= 2:
                        scalar.wait_ge(gSP[j].sem, 80 * (i // 2))
                    C["sVE"].inc(scalar.copy(out=vsb[j][:], in_=pv[j][:]))
                    if i + 2 < NT:
                        ev_T(i + 2)
                ev_fm(NT - 1)

            @block.tensor
            def _(tensor):
                tensor.wait_ge(C["sC"].sem, 144)
                tensor.wait_ge(C["vGF"].sem, VGF_TOT)

                def tp_fm(m):
                    jm = m % 2
                    tensor.wait_ge(C["vFMq"].sem, m + 1)
                    tensor.wait_ge(C["vFMk"].sem, m + 1)
                    srcs = (qhh[jm], qhl[jm], khh[jm], khl[jm])
                    for hh in range(16):
                        src = srcs[hh // 4]
                        col = 128 * (hh % 4)
                        p = C["tTP"].n + 1
                        assert p == POS_FM[m][0] + hh
                        if p > 2:
                            tensor.wait_ge(C["sEV"].sem, p - 2)
                        C["tTP"].inc(tensor.transpose(
                            pT[(p - 1) % 2].ap().bitcast(BF16)[:, 0:128],
                            src[:, col:col + 128], identb[:]))

                def tp_T(n):
                    jn = n % 2
                    tensor.wait_ge(C["vLNH"].sem, 4 * (n + 1))
                    for k in range(32):
                        srcT = lnxh[jn] if k < 16 else lnxl[jn]
                        kc = k % 16
                        p = C["tTP"].n + 1
                        assert p == POS_T[n][0] + k
                        if p > 2:
                            tensor.wait_ge(C["sEV"].sem, p - 2)
                        C["tTP"].inc(tensor.transpose(
                            pT[(p - 1) % 2].ap().bitcast(BF16)[:, 0:128],
                            srcT[:, 128 * kc:128 * (kc + 1)], identb[:]))

                tp_T(0)
                tp_T(1)
                for i in range(NT):
                    j = i % 2
                    tensor.wait_ge(C["sEV"].sem, POS_T[i][1])
                    if i >= 2:
                        tensor.wait_ge(C["vYQ"].sem, i - 1)
                        tensor.wait_ge(C["vYK"].sem, i - 1)
                        tensor.wait_ge(C["sVE"].sem, i - 1)
                    for dst, whi_t, wlo_t in ((pq[j], wqh, wql),
                                              (pk[j], wkh, wkl),
                                              (pv[j], wvh, wvl)):
                        ins = None
                        for ic in range(16):
                            for lh, wx in ((lnxTh[j], whi_t), (lnxTh[j], wlo_t),
                                           (lnxTl[j], whi_t)):
                                ins = tensor.matmul(
                                    dst[:], lh[:, 128 * ic:128 * (ic + 1)],
                                    wx[:, 512 * ic:512 * (ic + 1)],
                                    start=(ic == 0 and wx is whi_t and lh is lnxTh[j]),
                                    stop=(ic == 15 and lh is lnxTl[j]))
                        C["tPJ"].inc(ins)
                    if i >= 1:
                        tp_fm(i - 1)
                    if i + 2 < NT:
                        tp_T(i + 2)
                tp_fm(NT - 1)

            @block.gpsimd
            def _(gpsimd):
                for i in range(NT):
                    j = i % 2
                    gpsimd.wait_ge(C["sEV"].sem, POS_FM[i][1])
                    gpsimd.wait_ge(C["sVE"].sem, i + 1)
                    for dsp, ssp in ((qTh_sp, qhTh[j]), (qTl_sp, qhTl[j]),
                                     (kTh_sp, khTh[j]), (kTl_sp, khTl[j])):
                        gpsimd.dma_start(
                            out=dsp[:, :, 128 * i:128 * (i + 1)
                                    ].transpose([1, 0, 2]),
                            in_=ssp.ap().rearrange("p (h t) -> p h t", t=128),
                        ).then_inc(gSP[j].sem, 16)
                        gSP[j].n += 16
                    gpsimd.dma_start(
                        out=v_sp[:, 128 * i:128 * (i + 1), :].transpose([1, 0, 2]),
                        in_=vsb[j].ap().rearrange("p (h c) -> p h c", c=128),
                    ).then_inc(gSP[j].sem, 16)
                    gSP[j].n += 16

        p1f = {k: C[k].n for k in C}
        bar1 = [("tPJ", p1f["tPJ"]), ("sEV", p1f["sEV"]),
                ("vFMq", p1f["vFMq"]), ("vFMk", p1f["vFMk"]),
                ("sVE", p1f["sVE"]), ("vLNX", p1f["vLNX"]),
                ("gSP0", p1f["gSP0"]), ("gSP1", p1f["gSP1"]),
                ("tTP", p1f["tTP"]), ("vRCq", p1f["vRCq"]),
                ("vRCk", p1f["vRCk"]), ("sSDq", p1f["sSDq"]),
                ("sSDk", p1f["sSDk"])]

        with ExitStack() as p2:
            P = p2.enter_context
            KTh = P(nc.sbuf_tensor("KTh", [128, T], BF16))
            KTl = P(nc.sbuf_tensor("KTl", [128, T], BF16))
            Vsb = P(nc.sbuf_tensor("Vsb", [128, 32 * 129], F32))
            masks = P(nc.sbuf_tensor("maskssb", [128, 2048], F32))
            qtlh = [P(nc.sbuf_tensor(f"qtlh{j}", [128, 512], BF16)) for j in range(2)]
            qtll = [P(nc.sbuf_tensor(f"qtll{j}", [128, 512], BF16)) for j in range(2)]
            ssq = [P(nc.sbuf_tensor(f"ssq{j}", [128, 512], F32)) for j in range(2)]
            smk = [P(nc.sbuf_tensor(f"smk{j}", [128, 512], F32)) for j in range(2)]
            osb = [P(nc.sbuf_tensor(f"osb{j}", [128, 512], F32)) for j in range(2)]
            osbh = [P(nc.sbuf_tensor(f"osbh{j}", [128, 512], BF16)) for j in range(2)]
            osbl = [P(nc.sbuf_tensor(f"osbl{j}", [128, 512], BF16)) for j in range(2)]
            ssc = P(nc.sbuf_tensor("ssc", [128, 128], F32))
            rzt = P(nc.sbuf_tensor("rzt", [128, 4], F32))
            rz = P(nc.sbuf_tensor("rz", [128, 4], F32))
            pS = [P(nc.psum_tensor(f"pS{j}", [128, 512], F32)) for j in range(2)]
            pO = [P(nc.psum_tensor(f"pO{j}", [128, 512], F32)) for j in range(4)]


            @block.sync
            def _(sync):
                for nm, v in bar1:
                    sync.wait_ge(C[nm].sem, v)
                sync.dma_start(out=masks[:], in_=mk_in[:]).then_inc(C["sC"].sem, 16)
                C["sC"].n += 16
                r = 0
                for h in range(HPC):
                    if h > 0:
                        sync.wait_ge(C["tS"].sem, SPH * h)
                        sync.wait_ge(C["tOM"].sem, 4 * SPH * h)
                    sync.dma_start(out=KTh[:], in_=kTh_sp[h]
                                   ).then_inc(C["sKV"].sem, 16)
                    C["sKV"].n += 16
                    sync.dma_start(out=KTl[:], in_=kTl_sp[h]
                                   ).then_inc(C["sKV"].sem, 16)
                    C["sKV"].n += 16
                    sync.dma_start(
                        out=Vsb.ap().rearrange("p (kt c) -> p kt c", c=129)[:, :, 0:128],
                        in_=v_sp[h].rearrange("(kt p) c -> p kt c", p=128),
                    ).then_inc(C["sKV"].sem, 16)
                    C["sKV"].n += 16
                    for qt in range(NQ):
                        jr = r % 2
                        if r >= 2:
                            sync.wait_ge(C["tS"].sem, TS_AFTER_R[r - 2])
                        sync.dma_start(out=qtlh[jr][:],
                                       in_=qTh_sp[h][:, 512 * qt:512 * (qt + 1)]
                                       ).then_inc(sQ[jr].sem, 16)
                        sQ[jr].n += 16
                        sync.dma_start(out=qtll[jr][:],
                                       in_=qTl_sp[h][:, 512 * qt:512 * (qt + 1)]
                                       ).then_inc(sQ[jr].sem, 16)
                        sQ[jr].n += 16
                        r += 1

            @block.vector
            def _(vector):
                for nm, v in bar1:
                    vector.wait_ge(C[nm].sem, v)
                vector.wait_ge(C["sC"].sem, C["sC"].n)
                C["vMS"].inc(vector.memset(
                    Vsb.ap().rearrange("p (kt c) -> p kt c", c=129)[:, :, 128:129],
                    1.0))
                ck = 0
                prev_diag = [-10, -10]
                for h in range(HPC):
                    for qt in range(NQ):
                        r = 8 * h + qt
                        for kt in range(NKQ[qt]):
                            if kt >= 4 * qt:
                                jj = kt - 4 * qt
                                vector.wait_ge(C["sSQ"].sem, ck + 1)
                                pd = prev_diag[ck % 2]
                                if pd >= 0:
                                    vector.wait_ge(C["tOM"].sem, 4 * pd + 4)
                                C["vMK"].inc(vector.tensor_mul(
                                    smk[ck % 2][:], ssq[ck % 2][:],
                                    masks[:, 512 * jj:512 * (jj + 1)]))
                                prev_diag[ck % 2] = ck
                            ck += 1
                        vector.wait_ge(C["tOM"].sem, 4 * TS_AFTER_R[r])
                        if r >= 1:
                            vector.wait_ge(C["sOEb"].sem, 4 * r)
                        jr2 = r % 2
                        for s in range(4):
                            C["vRZ"].inc(vector.tensor_scalar(
                                rzt[:, s:s + 1], pO[s][:, 128:129],
                                EPS, None, ALU.add))
                        vector.wait_ge(C["vRZ"].sem, C["vRZ"].n)
                        for s in range(4):
                            C["vRZ"].inc(vector.reciprocal(rz[:, s:s + 1],
                                                           rzt[:, s:s + 1]))
                        for s in range(4):
                            cs = slice(128 * s, 128 * (s + 1))
                            vector.wait_ge(C["sOEb"].sem, 4 * r + s + 1)
                            vector.wait_ge(C["vRZ"].sem, C["vRZ"].n)
                            C["vRZ"].inc(vector.tensor_copy(ssc[:], osbh[jr2][:, cs]))
                            vector.wait_ge(C["vRZ"].sem, C["vRZ"].n)
                            C["vRZ"].inc(vector.tensor_sub(ssc[:], osb[jr2][:, cs],
                                                           ssc[:]))
                            vector.wait_ge(C["vRZ"].sem, C["vRZ"].n)
                            C["vRZ"].inc(vector.tensor_copy(osbl[jr2][:, cs], ssc[:]))

            @block.scalar
            def _(scalar):
                for nm, v in bar1:
                    scalar.wait_ge(C[nm].sem, v)
                ck = 0
                for h in range(HPC):
                    for qt in range(NQ):
                        r = 8 * h + qt
                        for kt in range(NKQ[qt]):
                            scalar.wait_ge(C["tS"].sem, ck + 1)
                            if ck >= 2:
                                scalar.wait_ge(C["tOM"].sem, 4 * (ck - 1))
                            C["sSQ"].inc(scalar.activation(
                                out=ssq[ck % 2][:], in_=pS[ck % 2][:],
                                func=AF.Square))
                            ck += 1
                        scalar.wait_ge(C["tOM"].sem, 4 * ck)
                        if r >= 2:
                            scalar.wait_ge(gO[r % 2].sem, 32 * (r // 2))
                        for s in range(4):
                            scalar.wait_ge(C["vRZ"].sem, 20 * r + 4 + s + 1)
                            cs = slice(128 * s, 128 * (s + 1))
                            scalar.activation(
                                out=osb[r % 2][:, cs],
                                in_=pO[s][:, 0:128], func=AF.Identity,
                                scale=rz[:, s:s + 1])
                            C["sOEb"].inc(scalar.activation(
                                out=osbh[r % 2][:, cs],
                                in_=pO[s][:, 0:128], func=AF.Identity,
                                scale=rz[:, s:s + 1]))

            @block.tensor
            def _(tensor):
                for nm, v in bar1:
                    tensor.wait_ge(C[nm].sem, v)
                tensor.wait_ge(C["vMS"].sem, 1)

                def emit_S(qt, kt, ck, r):
                    if kt == 0:
                        tensor.wait_ge(sQ[r % 2].sem, 32 * (r // 2 + 1))
                    if ck >= 2:
                        tensor.wait_ge(C["sSQ"].sem, ck - 1)
                    k0, k1 = 128 * kt, 128 * (kt + 1)
                    tensor.matmul(pS[ck % 2][:], KTh[:, k0:k1], qtlh[r % 2][:],
                                  start=True, stop=False)
                    tensor.matmul(pS[ck % 2][:], KTh[:, k0:k1], qtll[r % 2][:],
                                  start=False, stop=False)
                    C["tS"].inc(tensor.matmul(
                        pS[ck % 2][:], KTl[:, k0:k1], qtlh[r % 2][:],
                        start=False, stop=True))

                dk = 0
                ckh = 0
                for h in range(HPC):
                    tensor.wait_ge(C["sKV"].sem, 48 * (h + 1))
                    jobs = [(qt, kt) for qt in range(NQ) for kt in range(NKQ[qt])]
                    emit_S(jobs[0][0], jobs[0][1], ckh + 0, 8 * h)
                    emit_S(jobs[1][0], jobs[1][1], ckh + 1, 8 * h + jobs[1][0])
                    for idx, (qt, kt) in enumerate(jobs):
                        ck = ckh + idx
                        r = 8 * h + qt
                        nk = NKQ[qt]
                        if idx + 2 < len(jobs):
                            nqt, nkt = jobs[idx + 2]
                            emit_S(nqt, nkt, ck + 2, 8 * h + nqt)
                        if kt == 0 and r >= 1:
                            tensor.wait_ge(C["sOEb"].sem, 4 * r)
                        if kt >= 4 * qt:
                            dk += 1
                            tensor.wait_ge(C["vMK"].sem, dk)
                            src = smk[ck % 2]
                        else:
                            tensor.wait_ge(C["sSQ"].sem, ck + 1)
                            src = ssq[ck % 2]
                        for s in range(4):
                            C["tOM"].inc(tensor.matmul(
                                pO[s][:, 0:129],
                                src[:, 128 * s:128 * (s + 1)],
                                Vsb[:, 129 * kt:129 * kt + 129],
                                start=(kt == 0), stop=(kt == nk - 1)))
                    ckh += len(jobs)

            @block.gpsimd
            def _(gpsimd):
                for nm, v in bar1:
                    gpsimd.wait_ge(C[nm].sem, v)
                for h in range(HPC):
                    for qt in range(NQ):
                        r = 8 * h + qt
                        gpsimd.wait_ge(C["sOEb"].sem, 4 * r + 4)
                        gpsimd.wait_ge(C["vRZ"].sem, 20 * (r + 1))
                        for dsp, ssp in ((onh_sp, osbh[r % 2]),
                                         (onl_sp, osbl[r % 2])):
                            gpsimd.dma_start(
                                out=dsp[512 * qt:512 * (qt + 1),
                                        128 * h:128 * (h + 1)
                                        ].rearrange("(s p) c -> p s c", p=128),
                                in_=ssp.ap().rearrange("p (s c) -> p s c", c=128),
                            ).then_inc(gO[r % 2].sem, 16)
                            gO[r % 2].n += 16

        p2f = {k: C[k].n for k in C}
        bar2 = [("tOM", p2f["tOM"]), ("sOEb", p2f["sOEb"]),
                ("gO0", p2f["gO0"]), ("gO1", p2f["gO1"]),
                ("vMK", p2f["vMK"]), ("sSQ", p2f["sSQ"]),
                ("tS", p2f["tS"]), ("vRZ", p2f["vRZ"])]

        with ExitStack() as p25:
            P = p25.enter_context
            wo = P(nc.sbuf_tensor("wo", [128, 4 * DM], F32))
            identb2 = P(nc.sbuf_tensor("identb2", [D, D], BF16))
            woh = P(nc.sbuf_tensor("woh", [128, 4 * DM], BF16))
            wol = P(nc.sbuf_tensor("wol", [128, 4 * DM], BF16))
            onh = [P(nc.sbuf_tensor(f"onh{j}", [128, 512], BF16)) for j in range(2)]
            onl = [P(nc.sbuf_tensor(f"onl{j}", [128, 512], BF16)) for j in range(2)]
            oTh = [P(nc.sbuf_tensor(f"oTh{j}", [128, 512], BF16)) for j in range(2)]
            oTl = [P(nc.sbuf_tensor(f"oTl{j}", [128, 512], BF16)) for j in range(2)]
            pT2 = [P(nc.psum_tensor(f"pT2{j}", [128, 512], F32)) for j in range(2)]
            pM2 = [P(nc.psum_tensor(f"pM2{j}", [128, 512], F32)) for j in range(2)]



            @block.sync
            def _(sync):
                for nm, v in bar2:
                    sync.wait_ge(C[nm].sem, v)
                sync.dma_start(
                    out=wo.ap().rearrange("p (mi c) -> p mi c", c=DM),
                    in_=wot_in.rearrange("(mi p) c -> p mi c", p=128),
                ).then_inc(C["sWO"].sem, 16)
                C["sWO"].n += 16
                for idx in range(NT):
                    i = 8 * (idx % 4) + (idx // 4)
                    j = idx % 2
                    if idx >= 2:
                        sync.wait_ge(C["tTP2"].sem, 8 * (idx - 1))
                    sync.dma_start(out=onh[j][:],
                                   in_=onh_sp[128 * i:128 * (i + 1), :]
                                   ).then_inc(gX[j].sem, 16)
                    gX[j].n += 16
                    sync.dma_start(out=onl[j][:],
                                   in_=onl_sp[128 * i:128 * (i + 1), :]
                                   ).then_inc(gX[j].sem, 16)
                    gX[j].n += 16
                    if idx >= 2:
                        sync.wait_ge(C["vPR"].sem, 4 * (idx - 2) + 4)
                    sync.dma_start(out=s1[j][:],
                                   in_=x_in[128 * i:128 * (i + 1), :]
                                   ).then_inc(s1l[j].sem, 16)
                    s1l[j].n += 16

            @block.tensor
            def _(tensor):
                for nm, v in bar2:
                    tensor.wait_ge(C[nm].sem, v)
                tensor.wait_ge(C["vMS"].sem, 15)
                for idx in range(NT):
                    j = idx % 2
                    tensor.wait_ge(gX[j].sem, 32 * (idx // 2 + 1))
                    for k in range(8):
                        srcT = onh[j] if k < 4 else onl[j]
                        kc = k % 4
                        p = C["tTP2"].n + 1
                        if p > 2:
                            tensor.wait_ge(C["sEV2"].sem, p - 2)
                        C["tTP2"].inc(tensor.transpose(
                            pT2[(p - 1) % 2].ap().bitcast(BF16)[:, 0:128],
                            srcT[:, 128 * kc:128 * (kc + 1)], identb2[:]))
                    tensor.wait_ge(C["sEV2"].sem, 8 * (idx + 1))
                    for nt in range(4):
                        gidx = 4 * idx + nt
                        if gidx >= 2:
                            tensor.wait_ge(C["vPR"].sem, gidx - 1)
                        ins = None
                        for mi in range(4):
                            for ol, wx in ((oTh[idx % 2], woh), (oTh[idx % 2], wol),
                                           (oTl[idx % 2], woh)):
                                ins = tensor.matmul(
                                    pM2[gidx % 2][:],
                                    ol[:, 128 * mi:128 * (mi + 1)],
                                    wx[:, DM * mi + 512 * nt: DM * mi + 512 * (nt + 1)],
                                    start=(mi == 0 and ol is oTh[idx % 2] and wx is woh),
                                    stop=(mi == 3 and ol is oTl[idx % 2]))
                        C["tM25"].inc(ins)

            @block.scalar
            def _(scalar):
                for nm, v in bar2:
                    scalar.wait_ge(C[nm].sem, v)
                for idx in range(NT):
                    if idx >= 2:
                        scalar.wait_ge(C["tM25"].sem, 4 * (idx - 1))
                    for k in range(8):
                        p = 8 * idx + k + 1
                        scalar.wait_ge(C["tTP2"].sem, p)
                        dstT = oTh[idx % 2] if k < 4 else oTl[idx % 2]
                        kc = k % 4
                        C["sEV2"].inc(scalar.copy(
                            out=dstT[:, 128 * kc:128 * (kc + 1)],
                            in_=pT2[(p - 1) % 2].ap().bitcast(BF16)[:, 0:128]))

            @block.vector
            def _(vector):
                for nm, v in bar2:
                    vector.wait_ge(C[nm].sem, v)
                vector.wait_ge(C["sWO"].sem, 16)
                C["vMS"].inc(vector.tensor_copy(identb2[:], ident[:]))
                C["vMS"].inc(vector.tensor_copy(woh[:], wo[:]))
                for q4 in range(4):
                    vector.wait_ge(C["vMS"].sem, C["vMS"].n)
                    C["vMS"].inc(vector.tensor_copy(
                        s2[0][:], woh[:, 2048 * q4:2048 * (q4 + 1)]))
                    vector.wait_ge(C["vMS"].sem, C["vMS"].n)
                    C["vMS"].inc(vector.tensor_sub(
                        s2[0][:], wo[:, 2048 * q4:2048 * (q4 + 1)], s2[0][:]))
                    vector.wait_ge(C["vMS"].sem, C["vMS"].n)
                    C["vMS"].inc(vector.tensor_copy(
                        wol[:, 2048 * q4:2048 * (q4 + 1)], s2[0][:]))
                assert C["vMS"].n == 15, C["vMS"].n
                for idx in range(NT):
                    j = idx % 2
                    vector.wait_ge(s1l[j].sem, 16 * ((NT + idx) // 2 + 1))
                    for nt in range(4):
                        gidx = 4 * idx + nt
                        vector.wait_ge(C["tM25"].sem, gidx + 1)
                        if idx >= 2 and nt == 0:
                            vector.wait_ge(gPR[j].sem, 16 * (idx // 2))
                        C["vPR"].inc(vector.scalar_tensor_tensor(
                            out=s3[j][:, 512 * nt:512 * (nt + 1)],
                            in0=s1[j][:, 512 * nt:512 * (nt + 1)],
                            scalar=0.25, in1=pM2[gidx % 2][:],
                            op0=ALU.mult, op1=ALU.add))

            @block.gpsimd
            def _(gpsimd):
                for nm, v in bar2:
                    gpsimd.wait_ge(C[nm].sem, v)
                for idx in range(NT):
                    w, jj = idx % 8, idx // 8
                    i = 8 * (idx % 4) + (idx // 4)
                    m = (i % 8) // 2
                    rr = i // 8
                    sub = (i % 8) % 2
                    j = idx % 2
                    gpsimd.wait_ge(C["vPR"].sem, 4 * (idx + 1))
                    gpsimd.dma_start(
                        out=agin[m, 256 * rr + 128 * sub:
                                 256 * rr + 128 * (sub + 1), :],
                        in_=s3[j][:]).then_inc(gPR[j].sem, 16)
                    gPR[j].n += 16
                    done = (idx + 1) // 8
                    if (idx + 1) % 8 == 0:
                        mch = done - 1
                        gpsimd.wait_ge(gPR[0].sem, 16 * 4 * done)
                        gpsimd.wait_ge(gPR[1].sem, 16 * 4 * done)
                        gpsimd.collective_compute(
                            "ReduceScatter", ALU.add, replica_groups=groups,
                            ins=[agin[mch]], outs=[rsout[mch]],
                        ).then_inc(C["cCC"].sem, 1)
                        C["cCC"].n += 1

        p25f = {k: C[k].n for k in C}
        bar3 = [("tM25", p25f["tM25"]), ("vPR", p25f["vPR"]),
                ("sEV2", p25f["sEV2"]), ("tTP2", p25f["tTP2"]),
                ("gPR0", p25f["gPR0"]), ("gPR1", p25f["gPR1"]),
                ("sWO", p25f["sWO"])]

        S1B = 2 * NT
        TTP2B = 8 * NT
        SEV2B = 8 * NT

        LN_START = {0: 0, 1: 9}
        NORM_START = {}
        _cum = 18
        for _tt in range(NTT):
            if _tt + 2 < NTT:
                LN_START[_tt + 2] = _cum
                _cum += 9
            NORM_START[_tt] = _cum
            _cum += 9

        def LNSUB(n):
            return LN_START[n] + 9

        def VFIN(tt):
            return NORM_START[tt] + 9

        def VVN(tt):
            return NORM_START[tt] + 1

        def VONM(tt):
            return NORM_START[tt] + 5

        def VVPE4(n):
            return LN_START[n] + 4

        SCH_SD = {0: 1, 1: 2}
        SCH_SQ = {}
        _cum = 2
        for _tt in range(NTT):
            if _tt + 2 < NTT:
                _cum += 1
                SCH_SD[_tt + 2] = _cum
            SCH_SQ[_tt] = _cum + 1
            _cum += 3

        def SSQ5(tt):
            return SCH_SQ[tt]

        def SNU(tt):
            return SCH_SQ[tt] + 1

        def STH(tt):
            return SCH_SQ[tt] + 2

        def SSD4(n):
            return SCH_SD[n]

        def PT4(n):
            return (TTP2B + 32 * n + 1, TTP2B + 32 * n + 32)

        with ExitStack() as p4:
            P = p4.enter_context
            fscale = P(nc.sbuf_tensor("fscale", [128, DM], F32))
            x2r = P(nc.sbuf_tensor("x2r", [128, 4 * DM], F32))
            identb3 = P(nc.sbuf_tensor("identb3", [D, D], BF16))
            wstF = P(nc.sbuf_tensor("wstF", [128, 16 * DM], BF16))
            h2b = P(nc.sbuf_tensor("h2b", [128, DM], F32))
            h2h = [P(nc.sbuf_tensor(f"h2h{j}", [128, DM], BF16)) for j in range(2)]
            h2l = [P(nc.sbuf_tensor(f"h2l{j}", [128, DM], BF16)) for j in range(2)]
            h2Th = [P(nc.sbuf_tensor(f"h2Th{j}", [128, DM], BF16)) for j in range(2)]
            h2Tl = [P(nc.sbuf_tensor(f"h2Tl{j}", [128, DM], BF16)) for j in range(2)]
            pT4 = [P(nc.psum_tensor(f"pT4{j}", [128, 512], F32)) for j in range(2)]
            pM4 = [P(nc.psum_tensor(f"pM4{j}", [128, 512], F32)) for j in range(4)]

            @block.sync
            def _(sync):
                for nm, v in bar3:
                    sync.wait_ge(C[nm].sem, v)
                sync.dma_start(out=fscale[:], in_=fsc_in.partition_broadcast(128)
                               ).then_inc(C["sC"].sem, 16)
                C["sC"].n += 16
                for tt in range(NTT):
                    j = tt % 2
                    sync.wait_ge(C["cCC"].sem, tt // 2 + 1)
                    if tt >= 2:
                        sync.wait_ge(C["vH2"].sem, tt - 1)
                    sync.dma_start(out=s1[j][:],
                                   in_=rsout[tt // 2,
                                             128 * (tt % 2):128 * (tt % 2 + 1), :]
                                   ).then_inc(s1l[j].sem, 16)
                    s1l[j].n += 16

            @block.vector
            def _(vector):
                for nm, v in bar3:
                    vector.wait_ge(C[nm].sem, v)
                vector.wait_ge(C["sC"].sem, C["sC"].n)
                C["vMS"].inc(vector.tensor_copy(identb3[:], ident[:]))

                def emit_ln4(n):
                    jn = n % 2
                    xv = x2r[:, DM * (n % 4):DM * (n % 4 + 1)]
                    vector.wait_ge(s1l[jn].sem, 16 * ((S1B + n) // 2 + 1))
                    if n >= 4:
                        vector.wait_ge(C["vCH"].sem, VFIN(n - 4))
                    C["vCH"].inc(vector.tensor_copy(xv, s1[jn][:]))
                    vector.wait_ge(C["vCH"].sem, C["vCH"].n)
                    for q4 in range(3):
                        vector.bn_stats(st24[:, 6 * q4:6 * (q4 + 1)],
                                        xv[:, 512 * q4:512 * (q4 + 1)])
                    C["vCH"].inc(vector.bn_stats(st24[:, 18:24],
                                                 xv[:, 1536:2048]))
                    vector.wait_ge(C["vCH"].sem, C["vCH"].n)
                    C["vCH"].inc(vector.bn_aggr(mvln[:], st24[:]))
                    vector.wait_ge(C["vCH"].sem, C["vCH"].n)
                    C["vCH"].inc(vector.tensor_scalar(vpeln[:], mvln[:, 1:2],
                                                      EPS, None, ALU.add))
                    vector.wait_ge(C["sCH"].sem, SSD4(n))
                    C["vCH"].inc(vector.reciprocal(rsln[:], sdln[:]))
                    vector.wait_ge(C["vCH"].sem, C["vCH"].n)
                    C["vCH"].inc(vector.tensor_scalar(
                        h2b[:], xv, mvln[:, 0:1], rsln[:],
                        ALU.subtract, ALU.mult))
                    vector.wait_ge(C["vCH"].sem, C["vCH"].n)
                    if n >= 2:
                        vector.wait_ge(C["tTP2"].sem, PT4(n - 2)[1])
                    C["vCH"].inc(vector.tensor_copy(h2h[jn][:], h2b[:]))
                    vector.wait_ge(C["vCH"].sem, C["vCH"].n)
                    C["vCH"].inc(vector.tensor_copy(s1[jn][:], h2h[jn][:]))
                    vector.wait_ge(C["vCH"].sem, C["vCH"].n)
                    C["vCH"].inc(vector.tensor_sub(s1[jn][:], h2b[:], s1[jn][:]))
                    vector.wait_ge(C["vCH"].sem, C["vCH"].n)
                    C["vH2"].inc(vector.tensor_copy(h2l[jn][:], s1[jn][:]))

                emit_ln4(0)
                emit_ln4(1)
                for tt in range(NTT):
                    j = tt % 2
                    if tt + 2 < NTT:
                        emit_ln4(tt + 2)
                    for nt in range(4):
                        vector.wait_ge(C["tM4"].sem, 4 * tt + nt + 1)
                        C["vEV4"].inc(vector.tensor_mul(
                            s3[j][:, 512 * nt:512 * (nt + 1)], pM4[nt][:],
                            fscale[:, 512 * nt:512 * (nt + 1)]))
                    vector.wait_ge(C["sCH"].sem, SNU(tt))
                    C["vCH"].inc(vector.tensor_scalar(
                        sc4[:, 16 + tt:17 + tt],
                        fsc1[:, 4 * j + 1:4 * j + 2],
                        EPS_F, None, ALU.max))
                    vector.wait_ge(C["vCH"].sem, C["vCH"].n)
                    C["vCH"].inc(vector.tensor_scalar(
                        sc4[:, 2 * tt + 1:2 * tt + 2],
                        sc4[:, 16 + tt:17 + tt],
                        EPS_F, None, ALU.add))
                    vector.wait_ge(C["vCH"].sem, C["vCH"].n)
                    C["vCH"].inc(vector.reciprocal(
                        sc4[:, 24 + j:25 + j], sc4[:, 2 * tt + 1:2 * tt + 2]))
                    vector.wait_ge(C["vCH"].sem, C["vCH"].n)
                    vector.wait_ge(C["sCH"].sem, STH(tt))
                    C["vCH"].inc(vector.tensor_mul(
                        sc4[:, 26 + j:27 + j], sc4[:, 2 * tt:2 * tt + 1],
                        sc4[:, 24 + j:25 + j]))
                    vector.wait_ge(C["vCH"].sem, C["vCH"].n)
                    C["vCH"].inc(vector.tensor_scalar(
                        sc4[:, 28 + j:29 + j],
                        fsc1[:, 4 * j + 1:4 * j + 2],
                        sc4[:, 26 + j:27 + j], EPS_F, ALU.mult, ALU.max))
                    vector.wait_ge(C["vCH"].sem, C["vCH"].n)
                    C["vCH"].inc(vector.reciprocal(
                        sc4[:, 30 + j:31 + j], sc4[:, 28 + j:29 + j]))
                    vector.wait_ge(C["vCH"].sem, C["vCH"].n)
                    C["vCH"].inc(vector.tensor_scalar(
                        sc4[:, 30 + j:31 + j], sc4[:, 30 + j:31 + j],
                        0.99, 1.0, ALU.mult, ALU.min))
                    vector.wait_ge(C["vCH"].sem, C["vCH"].n)
                    C["vCH"].inc(vector.tensor_mul(
                        sc4[:, 26 + j:27 + j], sc4[:, 26 + j:27 + j],
                        sc4[:, 30 + j:31 + j]))
                    vector.wait_ge(C["vCH"].sem, C["vCH"].n)
                    if tt >= 2:
                        vector.wait_ge(s2l[j].sem, 16 * (tt // 2))
                    C["vCH"].inc(vector.scalar_tensor_tensor(
                        out=s2[j][:], in0=s3[j][:],
                        scalar=sc4[:, 26 + j:27 + j],
                        in1=x2r[:, DM * (tt % 4):DM * (tt % 4 + 1)],
                        op0=ALU.mult, op1=ALU.add))

            @block.scalar
            def _(scalar):
                for nm, v in bar3:
                    scalar.wait_ge(C[nm].sem, v)

                def sd_ln4(n):
                    scalar.wait_ge(C["vCH"].sem, VVPE4(n))
                    C["sCH"].inc(scalar.activation(out=sdln[:], in_=vpeln[:],
                                                   func=AF.Sqrt))

                def ev_T4(n):
                    jn = n % 2
                    p0 = PT4(n)[0]
                    for k in range(32):
                        tp_pos = p0 + k
                        scalar.wait_ge(C["tTP2"].sem, tp_pos)
                        dstT = h2Th[jn] if k < 16 else h2Tl[jn]
                        kc = k % 16
                        C["sEV2"].inc(scalar.copy(
                            out=dstT[:, 128 * kc:128 * (kc + 1)],
                            in_=pT4[(tp_pos - 1) % 2].ap().bitcast(BF16)[:, 0:128]))

                sd_ln4(0)
                sd_ln4(1)
                ev_T4(0)
                ev_T4(1)
                for tt in range(NTT):
                    j = tt % 2
                    if tt + 2 < NTT:
                        sd_ln4(tt + 2)
                    scalar.wait_ge(C["vEV4"].sem, 4 * tt + 4)
                    if tt >= 2:
                        scalar.wait_ge(s2l[tt % 2].sem, 16 * (tt // 2))
                        scalar.wait_ge(C["vCH"].sem, VONM(tt - 2))
                    C["sCH"].inc(scalar.activation(
                        out=s2[j][:], in_=s3[j][:], func=AF.Square,
                        accum_out=fsc1[:, 4 * j:4 * j + 1]))
                    scalar.wait_ge(C["sCH"].sem, C["sCH"].n)
                    C["sCH"].inc(scalar.activation(
                        out=fsc1[:, 4 * j + 1:4 * j + 2],
                        in_=fsc1[:, 4 * j:4 * j + 1], func=AF.Sqrt))
                    scalar.wait_ge(C["vCH"].sem, VVN(tt))
                    C["sCH"].inc(scalar.activation(
                        out=sc4[:, 2 * tt:2 * tt + 1],
                        in_=sc4[:, 16 + tt:17 + tt], func=AF.Tanh))
                    if tt + 2 < NTT:
                        ev_T4(tt + 2)

            @block.tensor
            def _(tensor):
                for nm, v in bar3:
                    tensor.wait_ge(C[nm].sem, v)
                tensor.wait_ge(C["vMS"].sem, 16)

                def tp_T4(n):
                    jn = n % 2
                    tensor.wait_ge(C["vH2"].sem, n + 1)
                    for k in range(32):
                        srcH = h2h[jn] if k < 16 else h2l[jn]
                        kc = k % 16
                        p = C["tTP2"].n + 1
                        assert p == PT4(n)[0] + k
                        if p > TTP2B + 2:
                            tensor.wait_ge(C["sEV2"].sem, p - 2)
                        C["tTP2"].inc(tensor.transpose(
                            pT4[(p - 1) % 2].ap().bitcast(BF16)[:, 0:128],
                            srcH[:, 128 * kc:128 * (kc + 1)], identb3[:]))

                tp_T4(0)
                tp_T4(1)
                for tt in range(NTT):
                    j = tt % 2
                    tensor.wait_ge(C["sEV2"].sem, PT4(tt)[1])
                    wsem = [wl[0], wl[1], gU[0], gU[1]]
                    for nt in range(4):
                        tensor.wait_ge(wsem[nt].sem, 16)
                        if tt >= 1:
                            tensor.wait_ge(C["vEV4"].sem, 4 * (tt - 1) + nt + 1)
                        ins = None
                        for hl in (h2Th[j], h2Tl[j]):
                            for mi in range(16):
                                ins = tensor.matmul(
                                    pM4[nt][:],
                                    hl[:, 128 * mi:128 * (mi + 1)],
                                    wstF[:, DM * mi + 512 * nt:
                                         DM * mi + 512 * (nt + 1)],
                                    start=(mi == 0 and hl is h2Th[j]),
                                    stop=(mi == 15 and hl is h2Tl[j]))
                        C["tM4"].inc(ins)
                    if tt + 2 < NTT:
                        tp_T4(tt + 2)

            @block.gpsimd
            def _(gpsimd):
                for nm, v in bar3:
                    gpsimd.wait_ge(C[nm].sem, v)
                wsem = [wl[0], wl[1], gU[0], gU[1]]
                for q4 in range(4):
                    gpsimd.dma_start(
                        out=wstF.ap().rearrange("p (mi c) -> p mi c",
                                                c=DM)[:, :, 512 * q4:512 * (q4 + 1)],
                        in_=wf_in[:, 512 * q4:512 * (q4 + 1)
                                  ].rearrange("(mi p) c -> p mi c", p=128),
                    ).then_inc(wsem[q4].sem, 16)
                    wsem[q4].n += 16
                for tt in range(NTT):
                    j = tt % 2
                    gpsimd.wait_ge(C["vCH"].sem, VFIN(tt))
                    gpsimd.dma_start(out=out[128 * tt:128 * (tt + 1), :],
                                     in_=s2[j][:]).then_inc(s2l[j].sem, 16)
                    s2l[j].n += 16
                gpsimd.wait_ge(s2l[0].sem, s2l[0].n)
                gpsimd.wait_ge(s2l[1].sem, s2l[1].n)

    return nc


_NC_CACHE = []


def kernel(**inputs):
    x = np.asarray(inputs["L0_attn_hidden_states"], dtype=np.float32)
    Wq = np.asarray(inputs["L0_attn_Wq"], dtype=np.float32)
    Wk = np.asarray(inputs["L0_attn_Wk"], dtype=np.float32)
    Wv = np.asarray(inputs["L0_attn_Wv"], dtype=np.float32)
    Wo = np.asarray(inputs["L0_attn_Wo"], dtype=np.float32)
    gam = np.asarray(inputs["L0_attn_gamma"], dtype=np.float32)
    bet = np.asarray(inputs["L0_attn_beta"], dtype=np.float32)
    Wf = np.asarray(inputs["L0_ffn_weight"], dtype=np.float32)
    fsc = np.asarray(inputs["L0_ffn_scale"], dtype=np.float32)

    ident = np.eye(128, dtype=np.float32)
    kk = np.arange(128)[:, None]
    qq = np.arange(512)[None, :]
    masks = np.concatenate(
        [(128 * jj + kk <= qq).astype(np.float32) for jj in range(4)], axis=1)

    if not _NC_CACHE:
        _NC_CACHE.append(build_nc())
    nc = _NC_CACHE[0]
    in_maps = []
    for c in range(8):
        b, gix = c // 4, c % 4
        in_maps.append({
            "x": np.ascontiguousarray(x[b]),
            "wqt": np.ascontiguousarray(Wq[512 * gix:512 * (gix + 1), :].T),
            "wkt": np.ascontiguousarray(Wk[512 * gix:512 * (gix + 1), :].T),
            "wvt": np.ascontiguousarray(Wv[512 * gix:512 * (gix + 1), :].T),
            "wot": np.ascontiguousarray(Wo[:, 512 * gix:512 * (gix + 1)].T),
            "wf": Wf,
            "gam": np.ascontiguousarray(gam.reshape(1, 128)),
            "bet": np.ascontiguousarray(bet.reshape(1, 128)),
            "fsc": np.ascontiguousarray(fsc.reshape(1, 2048)),
            "ident": ident,
            "masks": masks,
        })
    res = run_bass_kernel_spmd(nc, in_maps, core_ids=list(range(8)))
    outp = np.empty((2, 4096, 2048), dtype=np.float32)
    for c in range(8):
        outp[c // 4, 1024 * (c % 4):1024 * (c % 4 + 1)] = res.results[c]["out"]
    return outp


# version 15 (final)
# speedup vs baseline: 1.0538x; 1.0538x over previous
import numpy as np
from contextlib import ExitStack

import concourse.bass as bass
import concourse.mybir as mybir
from concourse.bass_utils import run_bass_kernel_spmd

F32 = mybir.dt.float32
BF16 = mybir.dt.bfloat16
AF = mybir.ActivationFunctionType
ALU = mybir.AluOpType

T = 4096
DM = 2048
D = 128
HPC = 4
NT = T // 128
NQ = T // 512
TS = 1024
NTT = TS // 128
EPS = 1e-5
EPS_F = 1e-7

NKQ = [4 * qt + 4 for qt in range(NQ)]
SPH = sum(NKQ)
QT_ORDER = [0, 2, 4, 6, 1, 3, 5, 7]
TS_AFTER_R = []
_c = 0
for _h in range(HPC):
    for _qt in QT_ORDER:
        _c += NKQ[_qt]
        TS_AFTER_R.append(_c)



class Ctr:
    __slots__ = ("sem", "n")

    def __init__(self, sem):
        self.sem = sem
        self.n = 0

    def inc(self, inst, k=1):
        inst.then_inc(self.sem, k)
        self.n += k
        return self.n


def build_nc():
    nc = bass.Bass("TRN2", target_bir_lowering=False)

    x_in = nc.dram_tensor("x", [T, DM], F32, kind="ExternalInput").ap()
    wqt_in = nc.dram_tensor("wqt", [DM, 512], F32, kind="ExternalInput").ap()
    wkt_in = nc.dram_tensor("wkt", [DM, 512], F32, kind="ExternalInput").ap()
    wvt_in = nc.dram_tensor("wvt", [DM, 512], F32, kind="ExternalInput").ap()
    wot_in = nc.dram_tensor("wot", [DM, DM], F32, kind="ExternalInput").ap()
    wf_in = nc.dram_tensor("wf", [DM, DM], F32, kind="ExternalInput").ap()
    gam_in = nc.dram_tensor("gam", [1, D], F32, kind="ExternalInput").ap()
    bet_in = nc.dram_tensor("bet", [1, D], F32, kind="ExternalInput").ap()
    fsc_in = nc.dram_tensor("fsc", [1, DM], F32, kind="ExternalInput").ap()
    id_in = nc.dram_tensor("ident", [D, D], F32, kind="ExternalInput").ap()
    mk_in = nc.dram_tensor("masks", [D, 2048], F32, kind="ExternalInput").ap()
    out = nc.dram_tensor("out", [TS, DM], F32, kind="ExternalOutput").ap()

    qTh_sp = nc.dram_tensor("qTh_sp", [HPC, D, T], BF16).ap()
    qTl_sp = nc.dram_tensor("qTl_sp", [HPC, D, T], BF16).ap()
    kTh_sp = nc.dram_tensor("kTh_sp", [HPC, D, T], BF16).ap()
    kTl_sp = nc.dram_tensor("kTl_sp", [HPC, D, T], BF16).ap()
    v_sp = nc.dram_tensor("v_sp", [HPC, T, D], F32).ap()
    agiH = [nc.dram_tensor(f"agiH{m}", [2048, 512], BF16).ap() for m in range(2)]
    agiL = [nc.dram_tensor(f"agiL{m}", [2048, 512], BF16).ap() for m in range(2)]
    agoH = [nc.dram_tensor(f"agoH{m}", [8192, 512], BF16).ap() for m in range(2)]
    agoL = [nc.dram_tensor(f"agoL{m}", [8192, 512], BF16).ap() for m in range(2)]
    x2sp = nc.dram_tensor("x2sp", [TS, DM], F32).ap()
    us = nc.dram_tensor("us", [TS, DM], F32).ap()

    groups = [[0, 1, 2, 3], [4, 5, 6, 7]]

    with ExitStack() as g:
        E = g.enter_context

        ident = E(nc.sbuf_tensor("identsb", [D, D], F32))
        s1 = [E(nc.sbuf_tensor(f"s1_{j}", [128, DM], F32)) for j in range(2)]
        s2 = [E(nc.sbuf_tensor(f"s2_{j}", [128, DM], F32)) for j in range(2)]
        s3 = [E(nc.sbuf_tensor(f"s3_{j}", [128, DM], F32)) for j in range(2)]
        st24 = E(nc.sbuf_tensor("st24", [128, 24], F32))
        mvln = E(nc.sbuf_tensor("mvln", [128, 2], F32))
        vpeln = E(nc.sbuf_tensor("vpeln", [128, 1], F32))
        sdln = E(nc.sbuf_tensor("sdln", [128, 1], F32))
        rsln = E(nc.sbuf_tensor("rsln", [128, 1], F32))
        sc4 = E(nc.sbuf_tensor("sc4", [128, 32], F32))
        fsc1 = E(nc.sbuf_tensor("fsc1", [128, 16], F32))

        names = [
            "sC", "sW", "s1l0", "s1l1", "s2l0", "s2l1", "vGF",
            "sWq0", "sWq1", "vLNH",
            "vST", "vAG", "vVPE", "vVPq", "vVPk", "sSD", "vRC", "vLNX",
            "tTP", "sEV", "tPJ", "sVE",
            "vYQ", "vSTq", "vAGq", "sSDq", "vRCq", "vFMq",
            "vYK", "vSTk", "vAGk", "sSDk", "vRCk", "vFMk",
            "gSP0", "gSP1",
            "sKV", "sQ0", "sQ1", "tS", "sSQ", "vMK", "tOM", "vRZ",
            "sOE", "sOEb", "gO0", "gO1", "vMS",
            "sWO", "tM25", "vPR", "gPR0", "gPR1", "cCC",
            "wl0", "wl1", "vCH", "sCH", "tM4", "vEV4",
            "gX0", "gX1", "gU0", "gU1", "gOUT", "vH2", "tTP2", "sEV2",
            "gXL0", "gXL1",
        ]
        C = {n: Ctr(E(nc.semaphore(n))) for n in names}
        s1l = [C["s1l0"], C["s1l1"]]
        s2l = [C["s2l0"], C["s2l1"]]
        gSP = [C["gSP0"], C["gSP1"]]
        sQ = [C["sQ0"], C["sQ1"]]
        gO = [C["gO0"], C["gO1"]]
        gPR = [C["gPR0"], C["gPR1"]]
        wl = [C["wl0"], C["wl1"]]
        gX = [C["gX0"], C["gX1"]]
        gU = [C["gU0"], C["gU1"]]
        gXL = [C["gXL0"], C["gXL1"]]

        block = E(nc.Block())

        with ExitStack() as p1:
            P = p1.enter_context
            wqh = P(nc.sbuf_tensor("wqh", [128, 16 * 512], BF16))
            wql = P(nc.sbuf_tensor("wql", [128, 16 * 512], BF16))
            wkh = P(nc.sbuf_tensor("wkh", [128, 16 * 512], BF16))
            wkl = P(nc.sbuf_tensor("wkl", [128, 16 * 512], BF16))
            wvh = P(nc.sbuf_tensor("wvh", [128, 16 * 512], BF16))
            wvl = P(nc.sbuf_tensor("wvl", [128, 16 * 512], BF16))
            grep = P(nc.sbuf_tensor("grep", [128, 512], F32))
            brep = P(nc.sbuf_tensor("brep", [128, 512], F32))
            identb = P(nc.sbuf_tensor("identb", [D, D], BF16))
            lnxTh = [P(nc.sbuf_tensor(f"lnxTh{j}", [128, DM], BF16)) for j in range(2)]
            lnxTl = [P(nc.sbuf_tensor(f"lnxTl{j}", [128, DM], BF16)) for j in range(2)]
            lnxh = [P(nc.sbuf_tensor(f"lnxh{j}", [128, DM], BF16)) for j in range(2)]
            lnxl = [P(nc.sbuf_tensor(f"lnxl{j}", [128, DM], BF16)) for j in range(2)]
            yq = P(nc.sbuf_tensor("yq", [128, 512], F32))
            yk = P(nc.sbuf_tensor("yk", [128, 512], F32))
            qhh = [P(nc.sbuf_tensor(f"qhh{j}", [128, 512], BF16)) for j in range(2)]
            qhl = [P(nc.sbuf_tensor(f"qhl{j}", [128, 512], BF16)) for j in range(2)]
            khh = [P(nc.sbuf_tensor(f"khh{j}", [128, 512], BF16)) for j in range(2)]
            khl = [P(nc.sbuf_tensor(f"khl{j}", [128, 512], BF16)) for j in range(2)]
            qhTh = [P(nc.sbuf_tensor(f"qhTh{j}", [128, 512], BF16)) for j in range(2)]
            qhTl = [P(nc.sbuf_tensor(f"qhTl{j}", [128, 512], BF16)) for j in range(2)]
            khTh = [P(nc.sbuf_tensor(f"khTh{j}", [128, 512], BF16)) for j in range(2)]
            khTl = [P(nc.sbuf_tensor(f"khTl{j}", [128, 512], BF16)) for j in range(2)]
            vsb = [P(nc.sbuf_tensor(f"vsb{j}", [128, 512], F32)) for j in range(2)]
            mvq = P(nc.sbuf_tensor("mvq", [128, 8], F32))
            mvk = P(nc.sbuf_tensor("mvk", [128, 8], F32))
            vpe4q = P(nc.sbuf_tensor("vpe4q", [128, 4], F32))
            vpe4k = P(nc.sbuf_tensor("vpe4k", [128, 4], F32))
            sd4q = P(nc.sbuf_tensor("sd4q", [128, 4], F32))
            sd4k = P(nc.sbuf_tensor("sd4k", [128, 4], F32))
            rs4q = P(nc.sbuf_tensor("rs4q", [128, 4], F32))
            rs4k = P(nc.sbuf_tensor("rs4k", [128, 4], F32))
            pq = [P(nc.psum_tensor(f"pq{j}", [128, 512], F32)) for j in range(2)]
            pk = [P(nc.psum_tensor(f"pk{j}", [128, 512], F32)) for j in range(2)]
            pv = [P(nc.psum_tensor(f"pv{j}", [128, 512], F32)) for j in range(2)]
            pT = [P(nc.psum_tensor(f"pT{j}", [128, 512], F32)) for j in range(2)]

            SCHED = [("T", 0), ("T", 1)]
            for _i in range(NT):
                if _i >= 1:
                    SCHED.append(("fm", _i - 1))
                if _i <= NT - 3:
                    SCHED.append(("T", _i + 2))
            SCHED.append(("fm", NT - 1))
            POS_T = {}
            POS_FM = {}
            _cum = 0
            for _kind, _idx in SCHED:
                _n = 32 if _kind == "T" else 16
                if _kind == "T":
                    POS_T[_idx] = (_cum + 1, _cum + _n)
                else:
                    POS_FM[_idx] = (_cum + 1, _cum + _n)
                _cum += _n
            TTP_TOTAL = _cum

            VGF_SUB = []
            _v = 1
            for _qi in range(12):
                _v += (4 if _qi < 8 else 0) + 3
                VGF_SUB.append(_v)
                _v += 1
            VGF_TOT = _v

            @block.sync
            def _(sync):
                for dst_, src_ in ((ident, id_in),):
                    sync.dma_start(out=dst_[:], in_=src_[:]).then_inc(C["sC"].sem, 16)
                    C["sC"].n += 16
                for r in range(4):
                    sync.dma_start(out=grep[:, 128 * r:128 * (r + 1)],
                                   in_=gam_in.partition_broadcast(128)
                                   ).then_inc(C["sC"].sem, 16)
                    C["sC"].n += 16
                    sync.dma_start(out=brep[:, 128 * r:128 * (r + 1)],
                                   in_=bet_in.partition_broadcast(128)
                                   ).then_inc(C["sC"].sem, 16)
                    C["sC"].n += 16
                wins = [wqt_in, wkt_in, wvt_in]
                sWq = [C["sWq0"], C["sWq1"]]
                for qi in range(12):
                    w, q = qi // 4, qi % 4
                    j = qi % 2
                    if qi >= 2:
                        sync.wait_ge(C["vGF"].sem, VGF_SUB[qi - 2])
                    sync.dma_start(
                        out=s3[j].ap().rearrange("p (icq c) -> p icq c", c=512),
                        in_=wins[w].rearrange("(ic p) c -> p ic c",
                                              p=128)[:, 4 * q:4 * (q + 1), :],
                    ).then_inc(sWq[j].sem, 16)
                    sWq[j].n += 16
                for i in range(NT):
                    j = i % 2
                    if i >= 2:
                        sync.wait_ge(C["vLNX"].sem, i - 1)
                    sync.dma_start(out=s1[j][:], in_=x_in[128 * i:128 * (i + 1), :]
                                   ).then_inc(s1l[j].sem, 16)
                    s1l[j].n += 16

            @block.vector
            def _(vector):
                vector.wait_ge(C["sC"].sem, C["sC"].n)
                C["vGF"].inc(vector.tensor_copy(identb[:], ident[:]))
                sWq = [C["sWq0"], C["sWq1"]]
                Whl = [(wqh, wql), (wkh, wkl), (wvh, wvl)]
                for qi in range(12):
                    w, q = qi // 4, qi % 4
                    j = qi % 2
                    hi_t, lo_t = Whl[w]
                    vector.wait_ge(sWq[j].sem, 16 * (qi // 2 + 1))
                    if qi < 8:
                        for c4 in range(4):
                            C["vGF"].inc(vector.tensor_mul(
                                s3[j][:, 512 * c4:512 * (c4 + 1)],
                                s3[j][:, 512 * c4:512 * (c4 + 1)], grep[:]))
                        vector.wait_ge(C["vGF"].sem, C["vGF"].n)
                    C["vGF"].inc(vector.tensor_copy(
                        hi_t[:, 2048 * q:2048 * (q + 1)], s3[j][:]))
                    vector.wait_ge(C["vGF"].sem, C["vGF"].n)
                    C["vGF"].inc(vector.tensor_copy(
                        s2[j][:], hi_t[:, 2048 * q:2048 * (q + 1)]))
                    vector.wait_ge(C["vGF"].sem, C["vGF"].n)
                    assert C["vGF"].n + 1 == VGF_SUB[qi] + 1 or True
                    C["vGF"].inc(vector.tensor_sub(s2[j][:], s3[j][:], s2[j][:]))
                    assert C["vGF"].n == VGF_SUB[qi], (qi, C["vGF"].n, VGF_SUB[qi])
                    vector.wait_ge(C["vGF"].sem, C["vGF"].n)
                    C["vGF"].inc(vector.tensor_copy(
                        lo_t[:, 2048 * q:2048 * (q + 1)], s2[j][:]))
                assert C["vGF"].n == VGF_TOT, (C["vGF"].n, VGF_TOT)

                def emit_ln(n):
                    jn = n % 2
                    xt = s1[jn]
                    vector.wait_ge(s1l[jn].sem, 16 * (n // 2 + 1))
                    for q4 in range(3):
                        vector.bn_stats(st24[:, 6 * q4:6 * (q4 + 1)],
                                        xt[:, 512 * q4:512 * (q4 + 1)])
                    C["vST"].inc(vector.bn_stats(st24[:, 18:24], xt[:, 1536:2048]))
                    vector.wait_ge(C["vST"].sem, C["vST"].n)
                    C["vAG"].inc(vector.bn_aggr(mvln[:], st24[:]))
                    vector.wait_ge(C["vAG"].sem, C["vAG"].n)
                    C["vVPE"].inc(vector.tensor_scalar(vpeln[:], mvln[:, 1:2],
                                                       EPS, None, ALU.add))
                    vector.wait_ge(C["sSD"].sem, n + 1)
                    C["vRC"].inc(vector.reciprocal(rsln[:], sdln[:]))
                    vector.wait_ge(C["vRC"].sem, C["vRC"].n)
                    C["vLNX"].inc(vector.tensor_scalar(
                        s2[jn][:], xt[:], mvln[:, 0:1], rsln[:],
                        ALU.subtract, ALU.mult))
                    vector.wait_ge(C["vLNX"].sem, C["vLNX"].n)
                    if n >= 2:
                        vector.wait_ge(C["tTP"].sem, POS_T[n - 2][1])
                    C["vLNH"].inc(vector.tensor_copy(lnxh[jn][:], s2[jn][:]))
                    vector.wait_ge(C["vLNH"].sem, C["vLNH"].n)
                    C["vLNH"].inc(vector.tensor_copy(s3[jn][:], lnxh[jn][:]))
                    vector.wait_ge(C["vLNH"].sem, C["vLNH"].n)
                    C["vLNH"].inc(vector.tensor_sub(s3[jn][:], s2[jn][:], s3[jn][:]))
                    vector.wait_ge(C["vLNH"].sem, C["vLNH"].n)
                    C["vLNH"].inc(vector.tensor_copy(lnxl[jn][:], s3[jn][:]))

                def emit_fm(i, which):
                    j = i % 2
                    if which == "q":
                        ps_, y_, mv_, vpe_, sd_, rs_ = \
                            pq[j], yq, mvq, vpe4q, sd4q, rs4q
                        hat_ = s3[j][:, 512:1024]
                        cY, cST, cAG, cVP, cSD, cRC, cFM = (
                            C["vYQ"], C["vSTq"], C["vAGq"], C["vVPq"],
                            C["sSDq"], C["vRCq"], C["vFMq"])
                        off = 1
                        scl = 128.0
                    else:
                        ps_, y_, mv_, vpe_, sd_, rs_ = \
                            pk[j], yk, mvk, vpe4k, sd4k, rs4k
                        hat_ = s3[j][:, 1024:1536]
                        cY, cST, cAG, cVP, cSD, cRC, cFM = (
                            C["vYK"], C["vSTk"], C["vAGk"], C["vVPk"],
                            C["sSDk"], C["vRCk"], C["vFMk"])
                        off = 2
                        scl = None
                    vector.wait_ge(C["tPJ"].sem, 3 * i + off)
                    cY.inc(vector.tensor_add(y_[:], ps_[:], brep[:]))
                    vector.wait_ge(cY.sem, cY.n)
                    ins = None
                    for h in range(4):
                        ins = vector.bn_stats(st24[:, 6 * h:6 * (h + 1)],
                                              y_[:, 128 * h:128 * (h + 1)])
                    cST.inc(ins)
                    vector.wait_ge(cST.sem, cST.n)
                    for h in range(4):
                        ins = vector.bn_aggr(mv_[:, 2 * h:2 * (h + 1)],
                                             st24[:, 6 * h:6 * (h + 1)])
                    cAG.inc(ins)
                    vector.wait_ge(cAG.sem, cAG.n)
                    if scl is not None:
                        cVP.inc(vector.tensor_scalar(
                            vpe_[:],
                            mv_.ap().rearrange("p (h t) -> p h t", t=2)[:, :, 1:2],
                            EPS, scl, ALU.add, ALU.mult))
                    else:
                        cVP.inc(vector.tensor_scalar(
                            vpe_[:],
                            mv_.ap().rearrange("p (h t) -> p h t", t=2)[:, :, 1:2],
                            EPS, None, ALU.add))
                    vector.wait_ge(cSD.sem, i + 1)
                    cRC.inc(vector.reciprocal(rs_[:], sd_[:]))
                    vector.wait_ge(cRC.sem, cRC.n)
                    if which == "q" and i >= 2:
                        vector.wait_ge(C["tTP"].sem, POS_FM[i - 2][1])
                    for h in range(4):
                        ins = vector.tensor_scalar(
                            hat_[:, 128 * h:128 * (h + 1)],
                            y_[:, 128 * h:128 * (h + 1)],
                            mv_[:, 2 * h:2 * h + 1], rs_[:, h:h + 1],
                            ALU.subtract, ALU.mult)
                    cRC.inc(ins)
                    vector.wait_ge(cRC.sem, cRC.n)
                    hi_b = qhh[j] if which == "q" else khh[j]
                    lo_b = qhl[j] if which == "q" else khl[j]
                    scr = s3[j][:, 1536:2048]
                    cRC.inc(vector.tensor_copy(hi_b[:], hat_[:]))
                    vector.wait_ge(cRC.sem, cRC.n)
                    cRC.inc(vector.tensor_copy(scr, hi_b[:]))
                    vector.wait_ge(cRC.sem, cRC.n)
                    cRC.inc(vector.tensor_sub(scr, hat_[:], scr))
                    vector.wait_ge(cRC.sem, cRC.n)
                    cFM.inc(vector.tensor_copy(lo_b[:], scr))

                emit_ln(0)
                emit_ln(1)
                for i in range(NT):
                    emit_fm(i, "q")
                    emit_fm(i, "k")
                    if i + 2 < NT:
                        emit_ln(i + 2)

            @block.scalar
            def _(scalar):
                def ev_fm(m):
                    jm = m % 2
                    if m >= 2:
                        scalar.wait_ge(gSP[jm].sem, 80 * (m // 2))
                    dsts = (qhTh[jm], qhTl[jm], khTh[jm], khTl[jm])
                    p0 = POS_FM[m][0]
                    for hh in range(16):
                        tp_pos = p0 + hh
                        scalar.wait_ge(C["tTP"].sem, tp_pos)
                        dst = dsts[hh // 4]
                        col = 128 * (hh % 4)
                        C["sEV"].inc(scalar.copy(
                            out=dst[:, col:col + 128],
                            in_=pT[(tp_pos - 1) % 2].ap().bitcast(BF16)[:, 0:128]))

                def ev_T(n):
                    jn = n % 2
                    if n >= 2:
                        scalar.wait_ge(C["tPJ"].sem, 3 * (n - 2) + 3)
                    p0 = POS_T[n][0]
                    for k in range(32):
                        tp_pos = p0 + k
                        scalar.wait_ge(C["tTP"].sem, tp_pos)
                        dstT = lnxTh[jn] if k < 16 else lnxTl[jn]
                        kc = k % 16
                        C["sEV"].inc(scalar.copy(
                            out=dstT[:, 128 * kc:128 * (kc + 1)],
                            in_=pT[(tp_pos - 1) % 2].ap().bitcast(BF16)[:, 0:128]))

                def sd_ln(n):
                    scalar.wait_ge(C["vVPE"].sem, n + 1)
                    C["sSD"].inc(scalar.activation(out=sdln[:], in_=vpeln[:],
                                                   func=AF.Sqrt))

                sd_ln(0)
                sd_ln(1)
                ev_T(0)
                ev_T(1)
                for i in range(NT):
                    j = i % 2
                    scalar.wait_ge(C["vVPq"].sem, i + 1)
                    C["sSDq"].inc(scalar.activation(
                        out=sd4q[:], in_=vpe4q[:], func=AF.Sqrt))
                    scalar.wait_ge(C["vVPk"].sem, i + 1)
                    C["sSDk"].inc(scalar.activation(
                        out=sd4k[:], in_=vpe4k[:], func=AF.Sqrt))
                    if i + 2 < NT:
                        sd_ln(i + 2)
                    if i >= 1:
                        ev_fm(i - 1)
                    scalar.wait_ge(C["tPJ"].sem, 3 * i + 3)
                    if i >= 2:
                        scalar.wait_ge(gSP[j].sem, 80 * (i // 2))
                    C["sVE"].inc(scalar.copy(out=vsb[j][:], in_=pv[j][:]))
                    if i + 2 < NT:
                        ev_T(i + 2)
                ev_fm(NT - 1)

            @block.tensor
            def _(tensor):
                tensor.wait_ge(C["sC"].sem, 144)
                tensor.wait_ge(C["vGF"].sem, VGF_TOT)

                def tp_fm(m):
                    jm = m % 2
                    tensor.wait_ge(C["vFMq"].sem, m + 1)
                    tensor.wait_ge(C["vFMk"].sem, m + 1)
                    srcs = (qhh[jm], qhl[jm], khh[jm], khl[jm])
                    for hh in range(16):
                        src = srcs[hh // 4]
                        col = 128 * (hh % 4)
                        p = C["tTP"].n + 1
                        assert p == POS_FM[m][0] + hh
                        if p > 2:
                            tensor.wait_ge(C["sEV"].sem, p - 2)
                        C["tTP"].inc(tensor.transpose(
                            pT[(p - 1) % 2].ap().bitcast(BF16)[:, 0:128],
                            src[:, col:col + 128], identb[:]))

                def tp_T(n):
                    jn = n % 2
                    tensor.wait_ge(C["vLNH"].sem, 4 * (n + 1))
                    for k in range(32):
                        srcT = lnxh[jn] if k < 16 else lnxl[jn]
                        kc = k % 16
                        p = C["tTP"].n + 1
                        assert p == POS_T[n][0] + k
                        if p > 2:
                            tensor.wait_ge(C["sEV"].sem, p - 2)
                        C["tTP"].inc(tensor.transpose(
                            pT[(p - 1) % 2].ap().bitcast(BF16)[:, 0:128],
                            srcT[:, 128 * kc:128 * (kc + 1)], identb[:]))

                tp_T(0)
                tp_T(1)
                for i in range(NT):
                    j = i % 2
                    tensor.wait_ge(C["sEV"].sem, POS_T[i][1])
                    if i >= 2:
                        tensor.wait_ge(C["vYQ"].sem, i - 1)
                        tensor.wait_ge(C["vYK"].sem, i - 1)
                        tensor.wait_ge(C["sVE"].sem, i - 1)
                    for dst, whi_t, wlo_t in ((pq[j], wqh, wql),
                                              (pk[j], wkh, wkl),
                                              (pv[j], wvh, wvl)):
                        ins = None
                        for ic in range(16):
                            for lh, wx in ((lnxTh[j], whi_t), (lnxTh[j], wlo_t),
                                           (lnxTl[j], whi_t)):
                                ins = tensor.matmul(
                                    dst[:], lh[:, 128 * ic:128 * (ic + 1)],
                                    wx[:, 512 * ic:512 * (ic + 1)],
                                    start=(ic == 0 and wx is whi_t and lh is lnxTh[j]),
                                    stop=(ic == 15 and lh is lnxTl[j]))
                        C["tPJ"].inc(ins)
                    if i >= 1:
                        tp_fm(i - 1)
                    if i + 2 < NT:
                        tp_T(i + 2)
                tp_fm(NT - 1)

            @block.gpsimd
            def _(gpsimd):
                for i in range(NT):
                    j = i % 2
                    gpsimd.wait_ge(C["sEV"].sem, POS_FM[i][1])
                    gpsimd.wait_ge(C["sVE"].sem, i + 1)
                    for dsp, ssp in ((qTh_sp, qhTh[j]), (qTl_sp, qhTl[j]),
                                     (kTh_sp, khTh[j]), (kTl_sp, khTl[j])):
                        gpsimd.dma_start(
                            out=dsp[:, :, 128 * i:128 * (i + 1)
                                    ].transpose([1, 0, 2]),
                            in_=ssp.ap().rearrange("p (h t) -> p h t", t=128),
                        ).then_inc(gSP[j].sem, 16)
                        gSP[j].n += 16
                    gpsimd.dma_start(
                        out=v_sp[:, 128 * i:128 * (i + 1), :].transpose([1, 0, 2]),
                        in_=vsb[j].ap().rearrange("p (h c) -> p h c", c=128),
                    ).then_inc(gSP[j].sem, 16)
                    gSP[j].n += 16

        p1f = {k: C[k].n for k in C}
        bar1 = [("tPJ", p1f["tPJ"]), ("sEV", p1f["sEV"]),
                ("vFMq", p1f["vFMq"]), ("vFMk", p1f["vFMk"]),
                ("sVE", p1f["sVE"]), ("vLNX", p1f["vLNX"]),
                ("gSP0", p1f["gSP0"]), ("gSP1", p1f["gSP1"]),
                ("tTP", p1f["tTP"]), ("vRCq", p1f["vRCq"]),
                ("vRCk", p1f["vRCk"]), ("sSDq", p1f["sSDq"]),
                ("sSDk", p1f["sSDk"])]

        with ExitStack() as p2:
            P = p2.enter_context
            KTh = P(nc.sbuf_tensor("KTh", [128, T], BF16))
            KTl = P(nc.sbuf_tensor("KTl", [128, T], BF16))
            Vsb = P(nc.sbuf_tensor("Vsb", [128, 32 * 129], F32))
            masks = P(nc.sbuf_tensor("maskssb", [128, 2048], F32))
            qtlh = [P(nc.sbuf_tensor(f"qtlh{j}", [128, 512], BF16)) for j in range(2)]
            qtll = [P(nc.sbuf_tensor(f"qtll{j}", [128, 512], BF16)) for j in range(2)]
            ssq = [P(nc.sbuf_tensor(f"ssq{j}", [128, 512], F32)) for j in range(2)]
            smk = [P(nc.sbuf_tensor(f"smk{j}", [128, 512], F32)) for j in range(2)]
            osb = [P(nc.sbuf_tensor(f"osb{j}", [128, 512], F32)) for j in range(2)]
            osbh = [P(nc.sbuf_tensor(f"osbh{j}", [128, 512], BF16)) for j in range(2)]
            osbl = [P(nc.sbuf_tensor(f"osbl{j}", [128, 512], BF16)) for j in range(2)]
            ssc = P(nc.sbuf_tensor("ssc", [128, 128], F32))
            rzt = P(nc.sbuf_tensor("rzt", [128, 4], F32))
            rz = P(nc.sbuf_tensor("rz", [128, 4], F32))
            pS = [P(nc.psum_tensor(f"pS{j}", [128, 512], F32)) for j in range(2)]
            pO = [P(nc.psum_tensor(f"pO{j}", [128, 512], F32)) for j in range(4)]


            @block.sync
            def _(sync):
                for nm, v in bar1:
                    sync.wait_ge(C[nm].sem, v)
                sync.dma_start(out=masks[:], in_=mk_in[:]).then_inc(C["sC"].sem, 16)
                C["sC"].n += 16
                r = 0
                for h in range(HPC):
                    if h > 0:
                        sync.wait_ge(C["tS"].sem, SPH * h)
                        sync.wait_ge(C["tOM"].sem, 4 * SPH * h)
                    sync.dma_start(out=KTh[:], in_=kTh_sp[h]
                                   ).then_inc(C["sKV"].sem, 16)
                    C["sKV"].n += 16
                    sync.dma_start(out=KTl[:], in_=kTl_sp[h]
                                   ).then_inc(C["sKV"].sem, 16)
                    C["sKV"].n += 16
                    sync.dma_start(
                        out=Vsb.ap().rearrange("p (kt c) -> p kt c", c=129)[:, :, 0:128],
                        in_=v_sp[h].rearrange("(kt p) c -> p kt c", p=128),
                    ).then_inc(C["sKV"].sem, 16)
                    C["sKV"].n += 16
                    for qt in QT_ORDER:
                        jr = r % 2
                        if r >= 2:
                            sync.wait_ge(C["tS"].sem, TS_AFTER_R[r - 2])
                        sync.dma_start(out=qtlh[jr][:],
                                       in_=qTh_sp[h][:, 512 * qt:512 * (qt + 1)]
                                       ).then_inc(sQ[jr].sem, 16)
                        sQ[jr].n += 16
                        sync.dma_start(out=qtll[jr][:],
                                       in_=qTl_sp[h][:, 512 * qt:512 * (qt + 1)]
                                       ).then_inc(sQ[jr].sem, 16)
                        sQ[jr].n += 16
                        r += 1

            @block.vector
            def _(vector):
                for nm, v in bar1:
                    vector.wait_ge(C[nm].sem, v)
                vector.wait_ge(C["sC"].sem, C["sC"].n)
                C["vMS"].inc(vector.memset(
                    Vsb.ap().rearrange("p (kt c) -> p kt c", c=129)[:, :, 128:129],
                    1.0))
                ck = 0
                prev_diag = [-10, -10]
                for h in range(HPC):
                    for qi, qt in enumerate(QT_ORDER):
                        r = 8 * h + qi
                        for kt in range(NKQ[qt]):
                            if kt >= 4 * qt:
                                jj = kt - 4 * qt
                                vector.wait_ge(C["sSQ"].sem, ck + 1)
                                pd = prev_diag[ck % 2]
                                if pd >= 0:
                                    vector.wait_ge(C["tOM"].sem, 4 * pd + 4)
                                C["vMK"].inc(vector.tensor_mul(
                                    smk[ck % 2][:], ssq[ck % 2][:],
                                    masks[:, 512 * jj:512 * (jj + 1)]))
                                prev_diag[ck % 2] = ck
                            ck += 1
                        vector.wait_ge(C["tOM"].sem, 4 * TS_AFTER_R[r])
                        if r >= 1:
                            vector.wait_ge(C["sOEb"].sem, 4 * r)
                        jr2 = r % 2
                        for s in range(4):
                            C["vRZ"].inc(vector.tensor_scalar(
                                rzt[:, s:s + 1], pO[s][:, 128:129],
                                EPS, None, ALU.add))
                        vector.wait_ge(C["vRZ"].sem, C["vRZ"].n)
                        for s in range(4):
                            C["vRZ"].inc(vector.reciprocal(rz[:, s:s + 1],
                                                           rzt[:, s:s + 1]))
                        for s in range(4):
                            cs = slice(128 * s, 128 * (s + 1))
                            vector.wait_ge(C["sOEb"].sem, 4 * r + s + 1)
                            vector.wait_ge(C["vRZ"].sem, C["vRZ"].n)
                            C["vRZ"].inc(vector.tensor_copy(ssc[:], osbh[jr2][:, cs]))
                            vector.wait_ge(C["vRZ"].sem, C["vRZ"].n)
                            C["vRZ"].inc(vector.tensor_sub(ssc[:], osb[jr2][:, cs],
                                                           ssc[:]))
                            vector.wait_ge(C["vRZ"].sem, C["vRZ"].n)
                            C["vRZ"].inc(vector.tensor_copy(osbl[jr2][:, cs], ssc[:]))

            @block.scalar
            def _(scalar):
                for nm, v in bar1:
                    scalar.wait_ge(C[nm].sem, v)
                ck = 0
                for h in range(HPC):
                    for qi, qt in enumerate(QT_ORDER):
                        r = 8 * h + qi
                        for kt in range(NKQ[qt]):
                            scalar.wait_ge(C["tS"].sem, ck + 1)
                            if ck >= 2:
                                scalar.wait_ge(C["tOM"].sem, 4 * (ck - 1))
                            C["sSQ"].inc(scalar.activation(
                                out=ssq[ck % 2][:], in_=pS[ck % 2][:],
                                func=AF.Square))
                            ck += 1
                        scalar.wait_ge(C["tOM"].sem, 4 * ck)
                        if r >= 2:
                            scalar.wait_ge(gO[r % 2].sem, 32 * (r // 2))
                        for s in range(4):
                            scalar.wait_ge(C["vRZ"].sem, 20 * r + 4 + s + 1)
                            cs = slice(128 * s, 128 * (s + 1))
                            scalar.activation(
                                out=osb[r % 2][:, cs],
                                in_=pO[s][:, 0:128], func=AF.Identity,
                                scale=rz[:, s:s + 1])
                            C["sOEb"].inc(scalar.activation(
                                out=osbh[r % 2][:, cs],
                                in_=pO[s][:, 0:128], func=AF.Identity,
                                scale=rz[:, s:s + 1]))

            @block.tensor
            def _(tensor):
                for nm, v in bar1:
                    tensor.wait_ge(C[nm].sem, v)
                tensor.wait_ge(C["vMS"].sem, 1)

                def emit_S(qt, kt, ck, r):
                    if kt == 0:
                        tensor.wait_ge(sQ[r % 2].sem, 32 * (r // 2 + 1))
                    if ck >= 2:
                        tensor.wait_ge(C["sSQ"].sem, ck - 1)
                    k0, k1 = 128 * kt, 128 * (kt + 1)
                    tensor.matmul(pS[ck % 2][:], KTh[:, k0:k1], qtlh[r % 2][:],
                                  start=True, stop=False)
                    tensor.matmul(pS[ck % 2][:], KTh[:, k0:k1], qtll[r % 2][:],
                                  start=False, stop=False)
                    C["tS"].inc(tensor.matmul(
                        pS[ck % 2][:], KTl[:, k0:k1], qtlh[r % 2][:],
                        start=False, stop=True))

                dk = 0
                ckh = 0
                for h in range(HPC):
                    tensor.wait_ge(C["sKV"].sem, 48 * (h + 1))
                    jobs = [(qi, qt, kt) for qi, qt in enumerate(QT_ORDER)
                            for kt in range(NKQ[qt])]
                    emit_S(jobs[0][1], jobs[0][2], ckh + 0, 8 * h + jobs[0][0])
                    emit_S(jobs[1][1], jobs[1][2], ckh + 1, 8 * h + jobs[1][0])
                    for idx, (qi, qt, kt) in enumerate(jobs):
                        ck = ckh + idx
                        r = 8 * h + qi
                        nk = NKQ[qt]
                        if idx + 2 < len(jobs):
                            _nqi, nqt, nkt = jobs[idx + 2]
                            emit_S(nqt, nkt, ck + 2, 8 * h + _nqi)
                        if kt == 0 and r >= 1:
                            tensor.wait_ge(C["sOEb"].sem, 4 * r)
                        if kt >= 4 * qt:
                            dk += 1
                            tensor.wait_ge(C["vMK"].sem, dk)
                            src = smk[ck % 2]
                        else:
                            tensor.wait_ge(C["sSQ"].sem, ck + 1)
                            src = ssq[ck % 2]
                        for s in range(4):
                            C["tOM"].inc(tensor.matmul(
                                pO[s][:, 0:129],
                                src[:, 128 * s:128 * (s + 1)],
                                Vsb[:, 129 * kt:129 * kt + 129],
                                start=(kt == 0), stop=(kt == nk - 1)))
                    ckh += len(jobs)

            @block.gpsimd
            def _(gpsimd):
                for nm, v in bar1:
                    gpsimd.wait_ge(C[nm].sem, v)
                for h in range(HPC):
                    for qi, qt in enumerate(QT_ORDER):
                        r = 8 * h + qi
                        m = 0 if qt % 2 == 0 else 1
                        row0 = 512 * (qt // 2)
                        gpsimd.wait_ge(C["sOEb"].sem, 4 * r + 4)
                        gpsimd.wait_ge(C["vRZ"].sem, 20 * (r + 1))
                        for dsp, ssp in ((agiH[m], osbh[r % 2]),
                                         (agiL[m], osbl[r % 2])):
                            gpsimd.dma_start(
                                out=dsp[row0:row0 + 512,
                                        128 * h:128 * (h + 1)
                                        ].rearrange("(s p) c -> p s c", p=128),
                                in_=ssp.ap().rearrange("p (s c) -> p s c", c=128),
                            ).then_inc(gO[r % 2].sem, 16)
                            gO[r % 2].n += 16
                        if h == HPC - 1:
                            cb = 2048 * (qt // 2)
                            gpsimd.wait_ge(gO[r % 2].sem, gO[r % 2].n)
                            gpsimd.collective_compute(
                                "AllGather", ALU.bypass, replica_groups=groups,
                                ins=[agiH[m][row0:row0 + 512, :]],
                                outs=[agoH[m][cb:cb + 2048, :]],
                            ).then_inc(C["cCC"].sem, 1)
                            C["cCC"].n += 1
                            gpsimd.collective_compute(
                                "AllGather", ALU.bypass, replica_groups=groups,
                                ins=[agiL[m][row0:row0 + 512, :]],
                                outs=[agoL[m][cb:cb + 2048, :]],
                            ).then_inc(C["cCC"].sem, 1)
                            C["cCC"].n += 1

        p2f = {k: C[k].n for k in C}
        bar2 = [("tOM", p2f["tOM"]), ("sOEb", p2f["sOEb"]),
                ("gO0", p2f["gO0"]), ("gO1", p2f["gO1"]),
                ("vMK", p2f["vMK"]), ("sSQ", p2f["sSQ"]),
                ("tS", p2f["tS"]), ("vRZ", p2f["vRZ"])]

        with ExitStack() as p25:
            P = p25.enter_context
            identb2 = P(nc.sbuf_tensor("identb2", [D, D], BF16))
            woFh = P(nc.sbuf_tensor("woFh", [128, 16 * DM], BF16))
            woFl = P(nc.sbuf_tensor("woFl", [128, 16 * DM], BF16))
            onhF = [s2[j].ap().bitcast(BF16)[:, 0:DM] for j in range(2)]
            onlF = [s2[j].ap().bitcast(BF16)[:, DM:2 * DM] for j in range(2)]
            oTFh = [P(nc.sbuf_tensor(f"oTFh{j}", [128, DM], BF16)) for j in range(2)]
            oTFl = [P(nc.sbuf_tensor(f"oTFl{j}", [128, DM], BF16)) for j in range(2)]
            pT2 = [P(nc.psum_tensor(f"pT2{j}", [128, 512], F32)) for j in range(2)]
            pM2 = [P(nc.psum_tensor(f"pM2{j}", [128, 512], F32)) for j in range(2)]
            NIX = NTT

            @block.sync
            def _(sync):
                for nm, v in bar2:
                    sync.wait_ge(C[nm].sem, v)
                wop = [C["sWO"], C["sW"]]
                for k in range(16):
                    j = k % 2
                    if k >= 2:
                        sync.wait_ge(C["vMS"].sem, 2 + 4 * (k - 2) + 3)
                    sync.dma_start(out=s2[j][:],
                                   in_=wot_in[128 * k:128 * (k + 1), :]
                                   ).then_inc(wop[j].sem, 16)
                    wop[j].n += 16

            @block.gpsimd
            def _(gpsimd):
                for nm, v in bar2:
                    gpsimd.wait_ge(C[nm].sem, v)
                gpsimd.wait_ge(C["vMS"].sem, 1 + 1 + 16 * 4)
                rkv = gpsimd.snap(nc.partition_id() % 4, min_val=0, max_val=3)
                sv = {}
                def ship_x2(idx):
                    j = idx % 2
                    gpsimd.wait_ge(C["vPR"].sem, 4 * (idx + 1))
                    gpsimd.dma_start(out=x2sp[128 * idx:128 * (idx + 1), :],
                                     in_=s3[j][:]).then_inc(gPR[j].sem, 16)
                    gPR[j].n += 16

                for rk in nc.Switch(engines=[gpsimd], index=[rkv], n=4):
                    if rk == 0:
                        sv["n"] = (gX[0].n, gX[1].n, gXL[0].n, gXL[1].n,
                                   gPR[0].n, gPR[1].n)
                    else:
                        (gX[0].n, gX[1].n, gXL[0].n, gXL[1].n,
                         gPR[0].n, gPR[1].n) = sv["n"]
                    for idx in range(NIX):
                        j = idx % 2
                        mc = idx // 4
                        qt_need = 2 * rk + mc
                        gpsimd.wait_ge(C["cCC"].sem,
                                       2 * QT_ORDER.index(qt_need) + 2)
                        if idx >= 2:
                            gpsimd.wait_ge(C["tTP2"].sem, 32 * (idx - 1))
                        for r in range(4):
                            cst = 2048 * rk + 512 * r + 128 * (idx % 4)
                            gpsimd.dma_start(
                                out=onhF[j][:, 512 * r:512 * (r + 1)],
                                in_=agoH[mc][cst:cst + 128, :],
                            ).then_inc(gX[j].sem, 16)
                            gX[j].n += 16
                            gpsimd.dma_start(
                                out=onlF[j][:, 512 * r:512 * (r + 1)],
                                in_=agoL[mc][cst:cst + 128, :],
                            ).then_inc(gX[j].sem, 16)
                            gX[j].n += 16
                        if idx >= 2:
                            gpsimd.wait_ge(C["vPR"].sem, 4 * (idx - 2) + 4)
                        xr = 1024 * rk + 128 * idx
                        gpsimd.dma_start(out=s1[j][:],
                                         in_=x_in[xr:xr + 128, :]
                                         ).then_inc(gXL[j].sem, 16)
                        gXL[j].n += 16
                        if idx >= 2:
                            ship_x2(idx - 2)
                for idx in range(NIX - 2, NIX):
                    ship_x2(idx)

            @block.vector
            def _(vector):
                for nm, v in bar2:
                    vector.wait_ge(C[nm].sem, v)
                C["vMS"].inc(vector.tensor_copy(identb2[:], ident[:]))
                wop = [C["sWO"], C["sW"]]
                for k in range(16):
                    j = k % 2
                    vector.wait_ge(wop[j].sem, 16 * (k // 2 + 1))
                    C["vMS"].inc(vector.tensor_copy(
                        woFh[:, DM * k:DM * (k + 1)], s2[j][:]))
                    vector.wait_ge(C["vMS"].sem, C["vMS"].n)
                    C["vMS"].inc(vector.tensor_copy(
                        s3[j][:], woFh[:, DM * k:DM * (k + 1)]))
                    vector.wait_ge(C["vMS"].sem, C["vMS"].n)
                    C["vMS"].inc(vector.tensor_sub(s3[j][:], s2[j][:], s3[j][:]))
                    vector.wait_ge(C["vMS"].sem, C["vMS"].n)
                    C["vMS"].inc(vector.tensor_copy(
                        woFl[:, DM * k:DM * (k + 1)], s3[j][:]))
                WOS_TOT = C["vMS"].n
                for idx in range(NIX):
                    j = idx % 2
                    vector.wait_ge(gXL[j].sem, 16 * (idx // 2 + 1))
                    for nt in range(4):
                        gidx = 4 * idx + nt
                        vector.wait_ge(C["tM25"].sem, gidx + 1)
                        if idx >= 2 and nt == 0:
                            vector.wait_ge(gPR[j].sem, 16 * (idx // 2))
                        C["vPR"].inc(vector.tensor_add(
                            s3[j][:, 512 * nt:512 * (nt + 1)],
                            s1[j][:, 512 * nt:512 * (nt + 1)],
                            pM2[gidx % 2][:]))

            @block.tensor
            def _(tensor):
                for nm, v in bar2:
                    tensor.wait_ge(C[nm].sem, v)
                tensor.wait_ge(C["vMS"].sem, 1 + 1 + 16 * 4)

                def tp_T25(n):
                    jn = n % 2
                    tensor.wait_ge(gX[jn].sem, 128 * (n // 2 + 1))
                    for k in range(32):
                        srcT = onhF[jn] if k < 16 else onlF[jn]
                        kc = k % 16
                        p = C["tTP2"].n + 1
                        if p > 2:
                            tensor.wait_ge(C["sEV2"].sem, p - 2)
                        C["tTP2"].inc(tensor.transpose(
                            pT2[(p - 1) % 2].ap().bitcast(BF16)[:, 0:128],
                            srcT[:, 128 * kc:128 * (kc + 1)], identb2[:]))

                tp_T25(0)
                tp_T25(1)
                for idx in range(NIX):
                    j = idx % 2
                    tensor.wait_ge(C["sEV2"].sem, 32 * (idx + 1))
                    for nt in range(4):
                        gidx = 4 * idx + nt
                        if gidx >= 2:
                            tensor.wait_ge(C["vPR"].sem, gidx - 1)
                        ins = None
                        for mi in range(16):
                            for ol, wx in ((oTFh[j], woFh), (oTFh[j], woFl),
                                           (oTFl[j], woFh)):
                                ins = tensor.matmul(
                                    pM2[gidx % 2][:],
                                    ol[:, 128 * mi:128 * (mi + 1)],
                                    wx[:, DM * mi + 512 * nt:
                                       DM * mi + 512 * (nt + 1)],
                                    start=(mi == 0 and ol is oTFh[j]
                                           and wx is woFh),
                                    stop=(mi == 15 and ol is oTFl[j]))
                        C["tM25"].inc(ins)
                    if idx + 2 < NIX:
                        tp_T25(idx + 2)

            @block.scalar
            def _(scalar):
                for nm, v in bar2:
                    scalar.wait_ge(C[nm].sem, v)

                def ev_T25(n):
                    if n >= 2:
                        scalar.wait_ge(C["tM25"].sem, 4 * (n - 1))
                    for k in range(32):
                        p = 32 * n + k + 1
                        scalar.wait_ge(C["tTP2"].sem, p)
                        dstT = oTFh[n % 2] if k < 16 else oTFl[n % 2]
                        kc = k % 16
                        C["sEV2"].inc(scalar.copy(
                            out=dstT[:, 128 * kc:128 * (kc + 1)],
                            in_=pT2[(p - 1) % 2].ap().bitcast(BF16)[:, 0:128]))

                ev_T25(0)
                ev_T25(1)
                for idx in range(NIX):
                    if idx + 2 < NIX:
                        ev_T25(idx + 2)

        p25f = {k: C[k].n for k in C}
        bar3 = [("tM25", p25f["tM25"]), ("vPR", p25f["vPR"]),
                ("sEV2", p25f["sEV2"]), ("tTP2", p25f["tTP2"]),
                ("gPR0", p25f["gPR0"]), ("gPR1", p25f["gPR1"]),
                ("sWO", p25f["sWO"])]

        S1B = NT
        TTP2B = 32 * NTT
        SEV2B = 32 * NTT

        LN_START = {0: 0, 1: 9}
        NORM_START = {}
        _cum = 18
        for _tt in range(NTT):
            if _tt + 2 < NTT:
                LN_START[_tt + 2] = _cum
                _cum += 9
            NORM_START[_tt] = _cum
            _cum += 9

        def LNSUB(n):
            return LN_START[n] + 9

        def VFIN(tt):
            return NORM_START[tt] + 9

        def VVN(tt):
            return NORM_START[tt] + 1

        def VONM(tt):
            return NORM_START[tt] + 5

        def VVPE4(n):
            return LN_START[n] + 4

        SCH_SD = {0: 1, 1: 2}
        SCH_SQ = {}
        _cum = 2
        for _tt in range(NTT):
            if _tt + 2 < NTT:
                _cum += 1
                SCH_SD[_tt + 2] = _cum
            SCH_SQ[_tt] = _cum + 1
            _cum += 3

        def SSQ5(tt):
            return SCH_SQ[tt]

        def SNU(tt):
            return SCH_SQ[tt] + 1

        def STH(tt):
            return SCH_SQ[tt] + 2

        def SSD4(n):
            return SCH_SD[n]

        def PT4(n):
            return (TTP2B + 32 * n + 1, TTP2B + 32 * n + 32)

        with ExitStack() as p4:
            P = p4.enter_context
            fscale = P(nc.sbuf_tensor("fscale", [128, DM], F32))
            x2r = P(nc.sbuf_tensor("x2r", [128, 4 * DM], F32))
            identb3 = P(nc.sbuf_tensor("identb3", [D, D], BF16))
            wstF = P(nc.sbuf_tensor("wstF", [128, 16 * DM], BF16))
            h2b = P(nc.sbuf_tensor("h2b", [128, DM], F32))
            h2h = [P(nc.sbuf_tensor(f"h2h{j}", [128, DM], BF16)) for j in range(2)]
            h2l = [P(nc.sbuf_tensor(f"h2l{j}", [128, DM], BF16)) for j in range(2)]
            h2Th = [P(nc.sbuf_tensor(f"h2Th{j}", [128, DM], BF16)) for j in range(2)]
            h2Tl = [P(nc.sbuf_tensor(f"h2Tl{j}", [128, DM], BF16)) for j in range(2)]
            pT4 = [P(nc.psum_tensor(f"pT4{j}", [128, 512], F32)) for j in range(2)]
            pM4 = [P(nc.psum_tensor(f"pM4{j}", [128, 512], F32)) for j in range(4)]

            @block.sync
            def _(sync):
                for nm, v in bar3:
                    sync.wait_ge(C[nm].sem, v)
                sync.dma_start(out=fscale[:], in_=fsc_in.partition_broadcast(128)
                               ).then_inc(C["sC"].sem, 16)
                C["sC"].n += 16
                for tt in range(NTT):
                    j = tt % 2
                    sync.wait_ge(gPR[j].sem, 16 * (tt // 2 + 1))
                    if tt >= 2:
                        sync.wait_ge(C["vH2"].sem, tt - 1)
                    sync.dma_start(out=s1[j][:],
                                   in_=x2sp[128 * tt:128 * (tt + 1), :]
                                   ).then_inc(s1l[j].sem, 16)
                    s1l[j].n += 16

            @block.vector
            def _(vector):
                for nm, v in bar3:
                    vector.wait_ge(C[nm].sem, v)
                vector.wait_ge(C["sC"].sem, C["sC"].n)
                C["vMS"].inc(vector.tensor_copy(identb3[:], ident[:]))

                def emit_ln4(n):
                    jn = n % 2
                    xv = x2r[:, DM * (n % 4):DM * (n % 4 + 1)]
                    vector.wait_ge(s1l[jn].sem, 16 * ((S1B + n) // 2 + 1))
                    if n >= 4:
                        vector.wait_ge(C["vCH"].sem, VFIN(n - 4))
                    C["vCH"].inc(vector.tensor_copy(xv, s1[jn][:]))
                    vector.wait_ge(C["vCH"].sem, C["vCH"].n)
                    for q4 in range(3):
                        vector.bn_stats(st24[:, 6 * q4:6 * (q4 + 1)],
                                        xv[:, 512 * q4:512 * (q4 + 1)])
                    C["vCH"].inc(vector.bn_stats(st24[:, 18:24],
                                                 xv[:, 1536:2048]))
                    vector.wait_ge(C["vCH"].sem, C["vCH"].n)
                    C["vCH"].inc(vector.bn_aggr(mvln[:], st24[:]))
                    vector.wait_ge(C["vCH"].sem, C["vCH"].n)
                    C["vCH"].inc(vector.tensor_scalar(vpeln[:], mvln[:, 1:2],
                                                      EPS, None, ALU.add))
                    vector.wait_ge(C["sCH"].sem, SSD4(n))
                    C["vCH"].inc(vector.reciprocal(rsln[:], sdln[:]))
                    vector.wait_ge(C["vCH"].sem, C["vCH"].n)
                    C["vCH"].inc(vector.tensor_scalar(
                        h2b[:], xv, mvln[:, 0:1], rsln[:],
                        ALU.subtract, ALU.mult))
                    vector.wait_ge(C["vCH"].sem, C["vCH"].n)
                    if n >= 2:
                        vector.wait_ge(C["tTP2"].sem, PT4(n - 2)[1])
                    C["vCH"].inc(vector.tensor_copy(h2h[jn][:], h2b[:]))
                    vector.wait_ge(C["vCH"].sem, C["vCH"].n)
                    C["vCH"].inc(vector.tensor_copy(s1[jn][:], h2h[jn][:]))
                    vector.wait_ge(C["vCH"].sem, C["vCH"].n)
                    C["vCH"].inc(vector.tensor_sub(s1[jn][:], h2b[:], s1[jn][:]))
                    vector.wait_ge(C["vCH"].sem, C["vCH"].n)
                    C["vH2"].inc(vector.tensor_copy(h2l[jn][:], s1[jn][:]))

                emit_ln4(0)
                emit_ln4(1)
                for tt in range(NTT):
                    j = tt % 2
                    if tt + 2 < NTT:
                        emit_ln4(tt + 2)
                    for nt in range(4):
                        vector.wait_ge(C["tM4"].sem, 4 * tt + nt + 1)
                        C["vEV4"].inc(vector.tensor_mul(
                            s3[j][:, 512 * nt:512 * (nt + 1)], pM4[nt][:],
                            fscale[:, 512 * nt:512 * (nt + 1)]))
                    vector.wait_ge(C["sCH"].sem, SNU(tt))
                    C["vCH"].inc(vector.tensor_scalar(
                        sc4[:, 16 + tt:17 + tt],
                        fsc1[:, 4 * j + 1:4 * j + 2],
                        EPS_F, None, ALU.max))
                    vector.wait_ge(C["vCH"].sem, C["vCH"].n)
                    C["vCH"].inc(vector.tensor_scalar(
                        sc4[:, 2 * tt + 1:2 * tt + 2],
                        sc4[:, 16 + tt:17 + tt],
                        EPS_F, None, ALU.add))
                    vector.wait_ge(C["vCH"].sem, C["vCH"].n)
                    C["vCH"].inc(vector.reciprocal(
                        sc4[:, 24 + j:25 + j], sc4[:, 2 * tt + 1:2 * tt + 2]))
                    vector.wait_ge(C["vCH"].sem, C["vCH"].n)
                    vector.wait_ge(C["sCH"].sem, STH(tt))
                    C["vCH"].inc(vector.tensor_mul(
                        sc4[:, 26 + j:27 + j], sc4[:, 2 * tt:2 * tt + 1],
                        sc4[:, 24 + j:25 + j]))
                    vector.wait_ge(C["vCH"].sem, C["vCH"].n)
                    C["vCH"].inc(vector.tensor_scalar(
                        sc4[:, 28 + j:29 + j],
                        fsc1[:, 4 * j + 1:4 * j + 2],
                        sc4[:, 26 + j:27 + j], EPS_F, ALU.mult, ALU.max))
                    vector.wait_ge(C["vCH"].sem, C["vCH"].n)
                    C["vCH"].inc(vector.reciprocal(
                        sc4[:, 30 + j:31 + j], sc4[:, 28 + j:29 + j]))
                    vector.wait_ge(C["vCH"].sem, C["vCH"].n)
                    C["vCH"].inc(vector.tensor_scalar(
                        sc4[:, 30 + j:31 + j], sc4[:, 30 + j:31 + j],
                        0.99, 1.0, ALU.mult, ALU.min))
                    vector.wait_ge(C["vCH"].sem, C["vCH"].n)
                    C["vCH"].inc(vector.tensor_mul(
                        sc4[:, 26 + j:27 + j], sc4[:, 26 + j:27 + j],
                        sc4[:, 30 + j:31 + j]))
                    vector.wait_ge(C["vCH"].sem, C["vCH"].n)
                    if tt >= 2:
                        vector.wait_ge(s2l[j].sem, 16 * (tt // 2))
                    C["vCH"].inc(vector.scalar_tensor_tensor(
                        out=s2[j][:], in0=s3[j][:],
                        scalar=sc4[:, 26 + j:27 + j],
                        in1=x2r[:, DM * (tt % 4):DM * (tt % 4 + 1)],
                        op0=ALU.mult, op1=ALU.add))

            @block.scalar
            def _(scalar):
                for nm, v in bar3:
                    scalar.wait_ge(C[nm].sem, v)

                def sd_ln4(n):
                    scalar.wait_ge(C["vCH"].sem, VVPE4(n))
                    C["sCH"].inc(scalar.activation(out=sdln[:], in_=vpeln[:],
                                                   func=AF.Sqrt))

                def ev_T4(n):
                    jn = n % 2
                    p0 = PT4(n)[0]
                    for k in range(32):
                        tp_pos = p0 + k
                        scalar.wait_ge(C["tTP2"].sem, tp_pos)
                        dstT = h2Th[jn] if k < 16 else h2Tl[jn]
                        kc = k % 16
                        C["sEV2"].inc(scalar.copy(
                            out=dstT[:, 128 * kc:128 * (kc + 1)],
                            in_=pT4[(tp_pos - 1) % 2].ap().bitcast(BF16)[:, 0:128]))

                sd_ln4(0)
                sd_ln4(1)
                ev_T4(0)
                ev_T4(1)
                for tt in range(NTT):
                    j = tt % 2
                    if tt + 2 < NTT:
                        sd_ln4(tt + 2)
                    scalar.wait_ge(C["vEV4"].sem, 4 * tt + 4)
                    if tt >= 2:
                        scalar.wait_ge(s2l[tt % 2].sem, 16 * (tt // 2))
                        scalar.wait_ge(C["vCH"].sem, VONM(tt - 2))
                    C["sCH"].inc(scalar.activation(
                        out=s2[j][:], in_=s3[j][:], func=AF.Square,
                        accum_out=fsc1[:, 4 * j:4 * j + 1]))
                    scalar.wait_ge(C["sCH"].sem, C["sCH"].n)
                    C["sCH"].inc(scalar.activation(
                        out=fsc1[:, 4 * j + 1:4 * j + 2],
                        in_=fsc1[:, 4 * j:4 * j + 1], func=AF.Sqrt))
                    scalar.wait_ge(C["vCH"].sem, VVN(tt))
                    C["sCH"].inc(scalar.activation(
                        out=sc4[:, 2 * tt:2 * tt + 1],
                        in_=sc4[:, 16 + tt:17 + tt], func=AF.Tanh))
                    if tt + 2 < NTT:
                        ev_T4(tt + 2)

            @block.tensor
            def _(tensor):
                for nm, v in bar3:
                    tensor.wait_ge(C[nm].sem, v)
                tensor.wait_ge(C["vMS"].sem, 16)

                def tp_T4(n):
                    jn = n % 2
                    tensor.wait_ge(C["vH2"].sem, n + 1)
                    for k in range(32):
                        srcH = h2h[jn] if k < 16 else h2l[jn]
                        kc = k % 16
                        p = C["tTP2"].n + 1
                        assert p == PT4(n)[0] + k
                        if p > TTP2B + 2:
                            tensor.wait_ge(C["sEV2"].sem, p - 2)
                        C["tTP2"].inc(tensor.transpose(
                            pT4[(p - 1) % 2].ap().bitcast(BF16)[:, 0:128],
                            srcH[:, 128 * kc:128 * (kc + 1)], identb3[:]))

                tp_T4(0)
                tp_T4(1)
                for tt in range(NTT):
                    j = tt % 2
                    tensor.wait_ge(C["sEV2"].sem, PT4(tt)[1])
                    wsem = [wl[0], wl[1], gU[0], gU[1]]
                    for nt in range(4):
                        tensor.wait_ge(wsem[nt].sem, 16)
                        if tt >= 1:
                            tensor.wait_ge(C["vEV4"].sem, 4 * (tt - 1) + nt + 1)
                        ins = None
                        for hl in (h2Th[j], h2Tl[j]):
                            for mi in range(16):
                                ins = tensor.matmul(
                                    pM4[nt][:],
                                    hl[:, 128 * mi:128 * (mi + 1)],
                                    wstF[:, DM * mi + 512 * nt:
                                         DM * mi + 512 * (nt + 1)],
                                    start=(mi == 0 and hl is h2Th[j]),
                                    stop=(mi == 15 and hl is h2Tl[j]))
                        C["tM4"].inc(ins)
                    if tt + 2 < NTT:
                        tp_T4(tt + 2)

            @block.gpsimd
            def _(gpsimd):
                for nm, v in bar3:
                    gpsimd.wait_ge(C[nm].sem, v)
                wsem = [wl[0], wl[1], gU[0], gU[1]]
                for q4 in range(4):
                    gpsimd.dma_start(
                        out=wstF.ap().rearrange("p (mi c) -> p mi c",
                                                c=DM)[:, :, 512 * q4:512 * (q4 + 1)],
                        in_=wf_in[:, 512 * q4:512 * (q4 + 1)
                                  ].rearrange("(mi p) c -> p mi c", p=128),
                    ).then_inc(wsem[q4].sem, 16)
                    wsem[q4].n += 16
                for tt in range(NTT):
                    j = tt % 2
                    gpsimd.wait_ge(C["vCH"].sem, VFIN(tt))
                    gpsimd.dma_start(out=out[128 * tt:128 * (tt + 1), :],
                                     in_=s2[j][:]).then_inc(s2l[j].sem, 16)
                    s2l[j].n += 16
                gpsimd.wait_ge(s2l[0].sem, s2l[0].n)
                gpsimd.wait_ge(s2l[1].sem, s2l[1].n)

    return nc


_NC_CACHE = []


def kernel(**inputs):
    x = np.asarray(inputs["L0_attn_hidden_states"], dtype=np.float32)
    Wq = np.asarray(inputs["L0_attn_Wq"], dtype=np.float32)
    Wk = np.asarray(inputs["L0_attn_Wk"], dtype=np.float32)
    Wv = np.asarray(inputs["L0_attn_Wv"], dtype=np.float32)
    Wo = np.asarray(inputs["L0_attn_Wo"], dtype=np.float32)
    gam = np.asarray(inputs["L0_attn_gamma"], dtype=np.float32)
    bet = np.asarray(inputs["L0_attn_beta"], dtype=np.float32)
    Wf = np.asarray(inputs["L0_ffn_weight"], dtype=np.float32)
    fsc = np.asarray(inputs["L0_ffn_scale"], dtype=np.float32)

    ident = np.eye(128, dtype=np.float32)
    kk = np.arange(128)[:, None]
    qq = np.arange(512)[None, :]
    masks = np.concatenate(
        [(128 * jj + kk <= qq).astype(np.float32) for jj in range(4)], axis=1)

    if not _NC_CACHE:
        _NC_CACHE.append(build_nc())
    nc = _NC_CACHE[0]
    in_maps = []
    for c in range(8):
        b, gix = c // 4, c % 4
        in_maps.append({
            "x": np.ascontiguousarray(x[b]),
            "wqt": np.ascontiguousarray(Wq[512 * gix:512 * (gix + 1), :].T),
            "wkt": np.ascontiguousarray(Wk[512 * gix:512 * (gix + 1), :].T),
            "wvt": np.ascontiguousarray(Wv[512 * gix:512 * (gix + 1), :].T),
            "wot": np.ascontiguousarray(Wo.T),
            "wf": Wf,
            "gam": np.ascontiguousarray(gam.reshape(1, 128)),
            "bet": np.ascontiguousarray(bet.reshape(1, 128)),
            "fsc": np.ascontiguousarray(fsc.reshape(1, 2048)),
            "ident": ident,
            "masks": masks,
        })
    res = run_bass_kernel_spmd(nc, in_maps, core_ids=list(range(8)))
    outp = np.empty((2, 4096, 2048), dtype=np.float32)
    for c in range(8):
        outp[c // 4, 1024 * (c % 4):1024 * (c % 4 + 1)] = res.results[c]["out"]
    return outp
